# version 15
# baseline (speedup 1.0000x reference)
import os
import contextlib
import numpy as np
import concourse.bass as bass
import concourse.mybir as mybir
from concourse.bass_utils import run_bass_kernel_spmd

F32 = mybir.dt.float32
BF16 = mybir.dt.bfloat16
AF = mybir.ActivationFunctionType
ALU = mybir.AluOpType
AX = mybir.AxisListType

NB = 2
S = 2048
D = 1024
H = 16
HD = 64
HP = H // 2
NE = 8
DF = 4096
T = NB * S
NDT = D // 128
EXP_BIAS = -20.0


class Sem:
    def __init__(self, h):
        self.h = h
        self.n = 0


class Prog:
    ENGINES = ("sync", "tensor", "vector", "scalar", "gpsimd")

    def __init__(self):
        self.q = {e: [] for e in self.ENGINES}
        self.selfsync = {}

    def op(self, eng, make, inc=None, inc_k=1):
        s = self.selfsync.get(eng)
        if s is not None:
            if s.n > 0:
                self.q[eng].append(lambda E, _h=s.h, _v=s.n: E.wait_ge(_h, _v))
            inc = s
            inc_k = 1
        if inc is not None:
            inc.n += inc_k
            h, k = inc.h, inc_k
            self.q[eng].append(lambda E, _m=make, _h=h, _k=k: _m(E).then_inc(_h, _k))
            return inc.n
        self.q[eng].append(lambda E, _m=make: _m(E))
        return None

    def opn(self, eng, make):
        assert eng not in self.selfsync
        self.q[eng].append(lambda E, _m=make: _m(E))

    def opf(self, eng, make):
        s = self.selfsync[eng]
        s.n += 1
        self.q[eng].append(lambda E, _m=make, _h=s.h: _m(E).then_inc(_h, 1))
        return s.n

    def wait(self, eng, sem, val):
        self.q[eng].append(lambda E, _h=sem.h, _v=val: E.wait_ge(_h, _v))

    def dma(self, eng, out, in_, inc):
        return self.op(eng, lambda E, _o=out, _i=in_: E.dma_start(out=_o, in_=_i), inc, 16)


def build(nc, stage="full", taps=()):
    p = Prog()
    tap_tensors = {}

    def dram(name, shape, dt, kind):
        return nc.dram_tensor(name, list(shape), dt, kind=kind).ap()

    x_d = dram("x", (T, D), F32, "ExternalInput")
    wqT_d = dram("wqT", (D, D), F32, "ExternalInput")
    wkT_d = dram("wkT", (D, D), F32, "ExternalInput")
    wvT_d = dram("wvT", (D, D), F32, "ExternalInput")
    woT_d = dram("woT", (D, D), F32, "ExternalInput")
    bo_d = dram("boc", (128, NDT), F32, "ExternalInput")
    ucol_d = dram("ucol", (128, HP), F32, "ExternalInput")
    vc1_d = dram("vc1", (128, HP), F32, "ExternalInput")
    vc2_d = dram("vc2", (128, HP), F32, "ExternalInput")
    tcc_d = dram("tcc", (128, S), F32, "ExternalInput")
    tnsns_d = dram("tnsns", (128, S), F32, "ExternalInput")
    pab_d = dram("pab", (64, S), F32, "ExternalInput")
    ident_d = dram("ident", (128, 128), BF16, "ExternalInput")
    wg_d = dram("wg", (D, NE), F32, "ExternalInput")
    w1_d = dram("w1", (NE, D, DF), F32, "ExternalInput")
    b1c_d = dram("b1c", (NE, 128, DF // 128), F32, "ExternalInput")
    w2_d = dram("w2", (NE, DF, D), F32, "ExternalInput")
    b2_d = dram("b2", (NE, D), F32, "ExternalInput")
    out_d = dram("out", (T, D), F32, "ExternalOutput")

    xnT_hbm = dram("xnT_scr", (D, T), BF16, "Internal")
    wvbf_hbm = dram("wvbf_scr", (D, D), BF16, "Internal")
    sT_hbm = dram("sT_scr", (D, T), BF16, "Internal")
    wgt_hbm = dram("wgt_scr", (NE, T), BF16, "Internal")

    def tap(name, shape, dt=F32):
        if name in taps:
            tap_tensors[name] = dram("tap_" + name, shape, dt, "ExternalOutput")
            return tap_tensors[name]
        return None

    tp_xnT = tap("xnT", (D, T), BF16)
    tp_wgt = tap("wgtT", (NE, T))
    tp_sT = tap("sT", (D, T), BF16)
    tp_acc = tap("acc", (D, T))

    es = contextlib.ExitStack()
    with es:
        es.enter_context(nc.allow_low_precision(
            reason="bf16 matmul operands with fp32 accumulation; bf16 staging within measured tolerance"))

        def sb(name, shape, dt):
            return es.enter_context(nc.sbuf_tensor(name, list(shape), dt))

        def sem(name):
            return Sem(es.enter_context(nc.semaphore(name)))

        DS = sem("DS")
        DG = sem("DG")
        DO = sem("DO")
        P = sem("P")
        A = sem("A")
        V = sem("V")
        sx = [sem("sx0"), sem("sx1")]
        sst = sem("sst")
        sxt = sem("sxt")
        sxn = [sem("sxn0"), sem("sxn1")]
        w1s = [sem("w1sa"), sem("w1sb")]
        w2s = [sem("w2sa"), sem("w2sb")]
        sgw = sem("sgw")
        smw = sem("smw")
        sb1s = [sem("sb1a"), sem("sb1b")]
        swr = sem("swr")
        ssr = [sem("ssr0"), sem("ssr1")]
        sso = [sem("sso0"), sem("sso1")]
        DOG = sem("DOG")
        sxc = [sem(f"sxc{i}") for i in range(4)]
        svq = [sem(f"svq{i}") for i in range(2)]
        svb = sem("svb")
        ssoc = [sem(f"ssoc{i}") for i in range(4)]
        p.selfsync = {"vector": V, "scalar": A}

        ident = sb("ident_s", (128, 128), BF16)
        bo_c = sb("bo_c", (128, NDT), F32)
        b2_b = sb("b2_b", (NE, D), BF16)
        ones1 = sb("ones1", (1, 128), BF16)
        cst = sb("cst", (128, 2), F32)

        p.dma("sync", ident[:], ident_d, DS)
        p.dma("sync", bo_c[:], bo_d, DS)
        p.dma("gpsimd", b2_b[:], b2_d, DG)
        p.op("vector", lambda E: E.memset(ones1[:], 1.0), V)
        p.op("vector", lambda E: E.memset(cst[:, 0:1], 1e-5), V)
        v_cst = p.op("vector", lambda E: E.memset(cst[:, 1:2], EXP_BIAS), V)
        p.wait("scalar", V, v_cst)

        last = {}

        es2 = contextlib.ExitStack()
        dg_xn = [None, None]
        with es2:
            def sb2(name, shape, dt):
                return es2.enter_context(nc.sbuf_tensor(name, list(shape), dt))

            wqT_b = sb2("wqT_b", (128, NDT * D), BF16)
            wkT_b = sb2("wkT_b", (128, NDT * D), BF16)
            woT_b = sb2("woT_b", (128, NDT * D), BF16)
            tcc = sb2("tcc_s", (128, S), BF16)
            tnsns = sb2("tnsns_s", (128, S), BF16)
            ucol = sb2("ucol_s", (128, HP), F32)
            vc1 = sb2("vc1_s", (128, HP), F32)
            vc2 = sb2("vc2_s", (128, HP), F32)
            wg_f = sb2("wg_f", (128, NDT * NE), F32)
            wg_hi = sb2("wg_hi", (128, NDT * NE), BF16)
            wg_lo = sb2("wg_lo", (128, NDT * NE), BF16)
            xt = [sb2(f"xt{i}", (128, D), F32) for i in range(2)]
            xnf_t = sb2("xnf_t", (128, D), F32)
            xnhi_t = [sb2(f"xnhi{i}", (128, D), BF16) for i in range(2)]
            xnlo_t = [sb2(f"xnlo{i}", (128, D), BF16) for i in range(2)]
            wgt_b = sb2("wgt_b", (NE, S), BF16)
            cols = sb2("cols", (128, 8), F32)
            bnst = sb2("bnst", (128, 12), F32)
            loT_t = [sb2(f"loT{i}", (128, NDT * 128), BF16) for i in range(2)]
            g_m1 = sb2("g_m1", (128, 1), F32)
            g_m2 = sb2("g_m2", (128, 1), F32)
            g_msk1 = sb2("g_msk1", (128, NE), F32)
            g_msk2 = sb2("g_msk2", (128, NE), F32)
            g_t2 = sb2("g_t2", (128, NE), F32)
            g_d = sb2("g_d", (128, 1), F32)
            g_e = sb2("g_e", (128, 1), F32)
            g_r = sb2("g_r", (128, 1), F32)
            g_w2 = sb2("g_w2", (128, 1), F32)
            g_t4 = sb2("g_t4", (128, NE), F32)
            g_wgt = sb2("g_wgt", (128, NE), BF16)
            xnT = sb2("xnT", (128, NDT * S), BF16)
            Kbuf = [sb2(f"Kbuf{i}", (128, S), BF16) for i in range(2)]
            Qbuf = [sb2(f"Qbuf{i}", (128, S), BF16) for i in range(2)]
            vaug = [[sb2(f"vaug{g}_{i}", (128, 16 * 65), BF16) for i in range(2)] for g in range(2)]
            vt = sb2("vt", (128, S), BF16)
            q_sb2 = [sb2(f"q_sb{i}", (128, 512), BF16) for i in range(2)]
            tmpX = sb2("tmpX", (128, 512), BF16)
            rope = sb2("rope", (128, 512), F32)
            exp_sb = [sb2(f"exp{i}", (128, 1024), BF16) for i in range(2)]
            rec_row = [sb2(f"rec_row{i}", (1, 1024), BF16) for i in range(2)]
            ctx_stage = [sb2(f"ctx_stage{i}", (64, 1024), BF16) for i in range(2)]
            ctx_buf = sb2("ctx_buf", (128, NDT * S), BF16)

            psA_ = es2.enter_context(nc.psum_tensor("psA", [128, 1024], F32))
            psB_ = es2.enter_context(nc.psum_tensor("psB", [128, 1024], F32))
            psC_ = es2.enter_context(nc.psum_tensor("psC", [128, 1024], F32))
            psD_ = es2.enter_context(nc.psum_tensor("psD", [128, 1024], F32))
            if os.environ.get("BASS_DEBUG_MEM"):
                print("SBUF free in attn scope:", nc.sbuf_bytes_remaining)
            psT = [psA_.ap().bitcast(BF16)[:, 0:128], psA_.ap().bitcast(BF16)[:, 1024:1152],
                   psB_.ap().bitcast(BF16)[:, 0:128], psB_.ap().bitcast(BF16)[:, 1024:1152]]
            psG = [psC_[:, 0:8], psC_[:, 512:520]]
            psW = psD_.ap().bitcast(BF16)[0:8, 0:128]

            wvq = [sb2(f"wvq{i}", (128, NDT * 256), BF16) for i in range(2)]
            p.dma("gpsimd", wvbf_hbm, wvT_d, svb)
            for w_sb_, w_d_ in ((wqT_b, wqT_d), (wkT_b, wkT_d), (woT_b, woT_d)):
                p.dma("gpsimd", w_sb_.ap().rearrange("p (dt n) -> p dt n", dt=NDT),
                      w_d_.rearrange("(dt p) n -> p dt n", p=128), DG)
            p.dma("gpsimd", tcc[:], tcc_d, DG)
            p.dma("gpsimd", tnsns[:], tnsns_d, DG)
            p.dma("sync", ucol[:], ucol_d, DS)
            p.dma("sync", vc1[:], vc1_d, DS)
            p.dma("sync", vc2[:], vc2_d, DS)
            p.dma("sync", wg_f.ap().rearrange("p (dt e) -> p dt e", dt=NDT),
                  wg_d.rearrange("(dt p) e -> p dt e", p=128), DS)
            DS_ALL = DS.n
            p.wait("vector", DS, DS_ALL)
            p.op("vector", lambda E: E.tensor_copy(wg_hi[:], wg_f[:]), V)
            p.op("vector", lambda E: E.tensor_tensor(wg_lo[:], wg_f[:], wg_hi[:], ALU.subtract), V)
            for kb in Kbuf:
                p.dma("gpsimd", kb[64:128, :], pab_d, DG)
            kb_DG = DG.n
            p.wait("tensor", DG, kb_DG)
            p.wait("tensor", DS, DS_ALL)
            v_ones = None
            for _g in range(2):
                for _h in range(2):
                    v_ones = p.op("vector", lambda E, _gg=_g, _hh=_h: E.memset(
                        vaug[_gg][_hh].ap().rearrange("p (j c) -> p j c", j=16)[:, :, 64:65], 1.0), V)

            def ln_gate_batch(b):
                base_tok = b * S
                p.wait("tensor", A, A.n)
                p.wait("tensor", V, V.n)
                if b > 0:
                    p.wait("scalar", sxt, dg_xn[b - 1])
                    p.wait("vector", sxt, dg_xn[b - 1])
                    p.wait("scalar", sgw, 16 * b)

                def issue_x(tt):
                    bf = tt % 2
                    if ("x", bf) in last:
                        p.wait("sync", V, last[("x", bf)])
                    return p.dma("sync", xt[bf][:], x_d[base_tok + tt * 128: base_tok + (tt + 1) * 128, :], sx[bf])

                dsx = {0: issue_x(0)}
                for tt in range(16):
                    buf = tt % 2
                    xtile = xt[buf]
                    if tt + 1 < 16:
                        dsx[tt + 1] = issue_x(tt + 1)
                    p.wait("vector", sx[buf], dsx[tt])
                    p.op("vector", lambda E, _x=xtile: E.bn_stats(bnst[:, 0:6], _x[:, 0:512]))
                    p.op("vector", lambda E, _x=xtile: E.bn_stats(bnst[:, 6:12], _x[:, 512:1024]))
                    if ("lnA",) in last:
                        p.wait("vector", A, last[("lnA",)])
                    vst = p.op("vector", lambda E: E.bn_aggr(
                        cols[:, 0:2], bnst.ap().rearrange("p (g s) -> p g s", g=2)), V)
                    p.wait("scalar", V, vst)
                    al = p.opf("scalar", lambda E: E.activation(cols[:, 6:7], cols[:, 1:2], AF.Ln, bias=cst[:, 0:1], scale=1.0))
                    last[("lnA",)] = al
                    ast = p.op("scalar", lambda E: E.activation(cols[:, 4:5], cols[:, 6:7], AF.Exp, bias=0.0, scale=-0.5), A)
                    p.wait("vector", A, ast)
                    p.op("vector", lambda E: E.tensor_scalar(cols[:, 5:6], cols[:, 0:1], cols[:, 4:5], -1.0, ALU.mult, ALU.mult))
                    if ("xnsrc", buf) in last:
                        p.wait("vector", P, last[("xnsrc", buf)])
                    xh, xl = xnhi_t[buf], xnlo_t[buf]
                    p.op("vector", lambda E, _x=xtile, _o=xh: E.tensor_scalar(
                        _o[:], _x[:], cols[:, 4:5], cols[:, 5:6], ALU.mult, ALU.add))
                    p.op("vector", lambda E, _x=xtile: E.tensor_scalar(
                        xnf_t[:], _x[:], cols[:, 4:5], cols[:, 5:6], ALU.mult, ALU.add))
                    v_xn = p.op("vector", lambda E, _o=xl, _hi=xh: E.tensor_tensor(_o[:], xnf_t[:], _hi[:], ALU.subtract), V)
                    last[("x", buf)] = v_xn
                    p.wait("tensor", V, v_xn)
                    pe_tr = None
                    for dt in range(NDT):
                        for half in range(2):
                            k4 = (dt * 2 + half) % 4
                            if ("psT", k4) in last:
                                eng_sem, cnt = last[("psT", k4)]
                                p.wait("tensor", eng_sem, cnt)
                            src = xh if half == 0 else xl
                            pe_tr = p.op("tensor", lambda E, _s=src, _dt=dt, _ps=psT[k4]:
                                         E.transpose(_ps, _s[:, _dt * 128:(_dt + 1) * 128], ident[:]), P)
                            if half == 0:
                                p.wait("scalar", P, pe_tr)
                                a_ev = p.opf("scalar", lambda E, _dt=dt, _tt=tt, _ps=psT[k4]:
                                             E.activation(xnT[:, _dt * S + _tt * 128: _dt * S + (_tt + 1) * 128],
                                                          _ps, AF.Copy))
                                last[("psT", k4)] = (A, a_ev)
                            else:
                                p.wait("vector", P, pe_tr)
                                v_ev = p.opf("vector", lambda E, _dt=dt, _b=buf, _ps=psT[k4]:
                                             E.tensor_copy(loT_t[_b][:, _dt * 128:(_dt + 1) * 128], _ps))
                                last[("psT", k4)] = (V, v_ev)
                    last[("xnsrc", buf)] = P.n
                    a_hi_done = A.n
                    v_lo_done = V.n
                    gps = psG[buf]
                    if ("gps", buf) in last:
                        p.wait("tensor", V, last[("gps", buf)])
                    p.wait("tensor", A, a_hi_done)
                    p.wait("tensor", V, v_lo_done)
                    specs = []
                    for lhs_kind, rhs in (("hi", wg_hi), ("hi", wg_lo), ("lo", wg_hi)):
                        for dt in range(NDT):
                            specs.append((lhs_kind, dt, rhs))
                    pe_g = None
                    for si, (lhs_kind, dt, rhs) in enumerate(specs):
                        st_, sp_ = (si == 0), (si == len(specs) - 1)
                        if lhs_kind == "hi":
                            mk = (lambda E, _dt=dt, _tt=tt, _r=rhs, _g=gps, _st=st_, _sp=sp_:
                                  E.matmul(_g, xnT[:, _dt * S + _tt * 128: _dt * S + (_tt + 1) * 128],
                                           _r[:, _dt * NE:(_dt + 1) * NE], start=_st, stop=_sp))
                        else:
                            mk = (lambda E, _dt=dt, _b=buf, _r=rhs, _g=gps, _st=st_, _sp=sp_:
                                  E.matmul(_g, loT_t[_b][:, _dt * 128:(_dt + 1) * 128],
                                           _r[:, _dt * NE:(_dt + 1) * NE], start=_st, stop=_sp))
                        if sp_:
                            pe_g = p.op("tensor", mk, P)
                        else:
                            p.opn("tensor", mk)
                    p.wait("vector", P, pe_g)
                    p.op("vector", lambda E, _g=gps: E.reduce_max(g_m1[:], _g, axis=AX.X))
                    p.op("vector", lambda E, _g=gps: E.tensor_scalar(g_msk1[:], _g, g_m1[:, 0:1], None, ALU.is_ge))
                    vg = p.op("vector", lambda E, _g=gps: E.scalar_tensor_tensor(
                        g_t2[:], g_msk1[:], -1e30, _g, ALU.mult, ALU.add), V)
                    last[("gps", buf)] = vg
                    p.op("vector", lambda E: E.reduce_max(g_m2[:], g_t2[:], axis=AX.X))
                    vg2 = p.op("vector", lambda E: E.tensor_tensor(g_d[:], g_m2[:], g_m1[:], ALU.subtract), V)
                    p.wait("scalar", V, vg2)
                    p.opf("scalar", lambda E: E.activation(g_w2[:], g_d[:], AF.Sigmoid, bias=0.0, scale=1.0))
                    ag = p.opf("scalar", lambda E: E.activation(g_r[:], g_d[:], AF.Sigmoid, bias=0.0, scale=-1.0))
                    p.wait("vector", A, ag)
                    p.op("vector", lambda E: E.tensor_scalar(g_msk2[:], g_t2[:], g_m2[:, 0:1], None, ALU.is_ge))
                    p.op("vector", lambda E: E.tensor_scalar(g_t4[:], g_msk1[:], g_r[:, 0:1], None, ALU.mult))
                    vwg = p.op("vector", lambda E: E.scalar_tensor_tensor(
                        g_wgt[:], g_msk2[:], g_w2[:, 0:1], g_t4[:], ALU.mult, ALU.add), V)
                    p.wait("tensor", V, vwg)
                    if "psW" in last:
                        p.wait("tensor", A, last["psW"])
                    peW = p.op("tensor", lambda E: E.transpose(psW, g_wgt[:], ident[:]), P)
                    p.wait("scalar", P, peW)
                    aW = p.opf("scalar", lambda E, _tt=tt:
                               E.activation(wgt_b[:, _tt * 128: (_tt + 1) * 128], psW, AF.Copy))
                    last["psW"] = aW
                    par = os.environ.get("BASS_PARANOID", "")
                    if "A" in par:
                        p.wait("vector", P, P.n); p.wait("vector", A, A.n)
                    if "B" in par:
                        p.wait("tensor", V, V.n); p.wait("tensor", A, A.n)
                    if "E" in par:
                        p.wait("tensor", V, V.n)
                    if "F" in par:
                        p.wait("tensor", A, A.n)
                    if "C" in par:
                        p.wait("scalar", P, P.n); p.wait("scalar", V, V.n)
                    if "D" in par:
                        p.wait("sync", P, P.n); p.wait("sync", A, A.n); p.wait("sync", V, V.n)
                        p.wait("gpsimd", P, P.n); p.wait("gpsimd", A, A.n); p.wait("gpsimd", V, V.n)
                p.wait("gpsimd", A, A.n)
                dg_xn[b] = p.dma("gpsimd",
                                 xnT_hbm.rearrange("(dt p) t -> p dt t", p=128)[:, :, base_tok:base_tok + S],
                                 xnT.ap().rearrange("p (dt t) -> p dt t", dt=NDT), sxt)
                p.wait("sync", A, A.n)
                p.dma("sync", wgt_hbm[:, base_tok:base_tok + S], wgt_b[:], sgw)

            def emit_pend():
                (t_psc, t_par, t_vr, t_acs, t_head, t_ig) = last[("pend",)]
                p.wait("tensor", V, t_vr)
                p.wait("tensor", A, t_acs)
                pe_b = None
                for nh in range(2):
                    mk = (lambda E, _nh=nh, _pc=t_psc, _par=t_par:
                          E.matmul(_pc[0:64, _nh * 512:(_nh + 1) * 512], ones1[0:1, 0:64],
                                   rec_row[_par][:, _nh * 512:(_nh + 1) * 512], start=True, stop=True))
                    if nh == 1:
                        pe_b = p.op("tensor", mk, P)
                    else:
                        p.opn("tensor", mk)
                p.wait("vector", P, pe_b)
                v_cb = p.opf("vector", lambda E, _pc=t_psc, _par=t_par, _hh=t_head // 2, _tig=t_ig, _ro=64 * (t_head % 2):
                             E.tensor_tensor(
                                 ctx_buf[_ro:_ro + 64, _hh * S + _tig * 1024:_hh * S + (_tig + 1) * 1024],
                                 ctx_stage[_par][:], _pc[0:64, :], ALU.mult))
                last[("ps_c", t_par)] = (v_cb, t_acs)
                last[("ctxst", t_par)] = v_cb
                last[("rec", t_par)] = pe_b
                del last[("pend",)]

            def wv_load(b, qi):
                p.wait("sync", svb, 16)
                if ("wvu", qi % 2) in last:
                    p.wait("sync", P, last[("wvu", qi % 2)])
                last[("wvl", qi % 2)] = p.dma(
                    "sync", wvq[qi % 2].ap().rearrange("p (dt n) -> p dt n", dt=NDT),
                    wvbf_hbm.rearrange("(dt p) n -> p dt n", p=128)[:, :, qi * 256:(qi + 1) * 256],
                    svq[qi % 2])

            def attention_batch(b):
                base_tok = b * S
                p.wait("tensor", A, A.n)
                p.wait("tensor", V, V.n)
                wv_load(b, 0)
                wv_load(b, 1)
                if ("sTst", 0) in last:
                    p.wait("vector", sst, last[("sTst", 0)])
                for hp_i in range(HP):
                    if ("slot_exp",) in last:
                        p.wait("tensor", A, last[("slot_exp",)])
                    for ic in range(4):
                        ps_kq = psA_ if ic < 2 else psB_
                        off = (ic % 2) * 512
                        if ("psKQ", ic) in last:
                            a_c, v_c = last[("psKQ", ic)]
                            if a_c is not None:
                                p.wait("tensor", A, a_c)
                            if v_c is not None:
                                p.wait("tensor", V, v_c)
                        pe_k = None
                        for dt in range(NDT):
                            mk = (lambda E, _dt=dt, _h=hp_i, _ic=ic, _ps=ps_kq, _o=off:
                                  E.matmul(_ps[:, _o:_o + 512],
                                           wkT_b[:, _dt * D + _h * 128:_dt * D + (_h + 1) * 128],
                                           xnT[:, _dt * S + _ic * 512: _dt * S + (_ic + 1) * 512],
                                           start=(_dt == 0), stop=(_dt == NDT - 1)))
                            if dt == NDT - 1:
                                pe_k = p.op("tensor", mk, P)
                            else:
                                p.opn("tensor", mk)
                        p.wait("scalar", P, pe_k)
                        a_k = p.opf("scalar", lambda E, _ic=ic, _ps=ps_kq, _o=off:
                                    E.activation(Kbuf[0][0:64, _ic * 512:(_ic + 1) * 512],
                                                 _ps[0:64, _o:_o + 512], AF.Copy))
                        p.wait("vector", P, pe_k)
                        v_k = p.opf("vector", lambda E, _ic=ic, _ps=ps_kq, _o=off:
                                    E.tensor_copy(Kbuf[1][0:64, _ic * 512:(_ic + 1) * 512],
                                                  _ps[64:128, _o:_o + 512]))
                        last[("psKQ", ic)] = (a_k, v_k)
                    if ("pend",) in last:
                        emit_pend()
                    if hp_i % 2 == 0:
                        qi = hp_i // 2
                        if ("ps_c", 0) in last:
                            v_cbp0, a_csp0 = last[("ps_c", 0)]
                            p.wait("tensor", V, v_cbp0)
                            p.wait("tensor", A, a_csp0)
                        pe_v = None
                        for jt in range(16):
                            half = jt % 2
                            coff = 512 * half
                            if ("psCv", half) in last:
                                p.wait("tensor", A, last[("psCv", half)])
                            if jt == 0:
                                p.wait("tensor", svq[qi % 2], last[("wvl", qi % 2)])
                            for dt in range(NDT):
                                mk = (lambda E, _dt=dt, _jt=jt, _c=coff, _w=wvq[qi % 2]:
                                      E.matmul(psC_[:, _c:_c + 256],
                                               xnT[:, _dt * S + _jt * 128: _dt * S + (_jt + 1) * 128],
                                               _w[:, _dt * 256:(_dt + 1) * 256],
                                               start=(_dt == 0), stop=(_dt == NDT - 1)))
                                if dt == NDT - 1:
                                    pe_v = p.op("tensor", mk, P)
                                else:
                                    p.opn("tensor", mk)
                            p.wait("scalar", P, pe_v)
                            a_v = None
                            for gp in range(2):
                                p.opf("scalar", lambda E, _jt=jt, _c=coff, _gp=gp:
                                      E.activation(vaug[_gp][0][:, _jt * 65:_jt * 65 + 64],
                                                   psC_[:, _c + _gp * 128:_c + _gp * 128 + 64], AF.Copy))
                                a_v = p.opf("scalar", lambda E, _jt=jt, _c=coff, _gp=gp:
                                            E.activation(vaug[_gp][1][:, _jt * 65:_jt * 65 + 64],
                                                         psC_[:, _c + _gp * 128 + 64:_c + _gp * 128 + 128], AF.Copy))
                            last[("psCv", half)] = a_v
                        last[("wvu", qi % 2)] = pe_v
                        if qi + 2 <= 3:
                            wv_load(b, qi + 2)
                        last[("avdone",)] = A.n
                    a_v_done = last[("avdone",)]
                    for icq in range(4):
                        p.op("vector", lambda E, _h=hp_i, _i=icq: E.tensor_scalar(
                            vt[:, _i * 512:(_i + 1) * 512], tcc[:, _i * 512:(_i + 1) * 512],
                            vc1[:, _h:_h + 1], None, ALU.mult))
                        p.op("vector", lambda E, _h=hp_i, _i=icq: E.tensor_scalar(
                            tmpX[:], tnsns[:, _i * 512:(_i + 1) * 512], vc2[:, _h:_h + 1], None, ALU.mult))
                        p.op("vector", lambda E, _i=icq: E.tensor_tensor(
                            vt[:, _i * 512:(_i + 1) * 512], vt[:, _i * 512:(_i + 1) * 512], tmpX[:], ALU.subtract))
                    a_q_l = [None] * 4
                    v_q_ic = [None] * 4
                    for ic in range(4):
                        ps_kq = psA_ if ic < 2 else psB_
                        off = (ic % 2) * 512
                        a_c, v_c = last[("psKQ", ic)]
                        if a_c is not None:
                            p.wait("tensor", A, a_c)
                        if v_c is not None:
                            p.wait("tensor", V, v_c)
                        pe_q = None
                        for dt in range(NDT):
                            mk = (lambda E, _dt=dt, _h=hp_i, _ic=ic, _ps=ps_kq, _o=off:
                                  E.matmul(_ps[:, _o:_o + 512],
                                           wqT_b[:, _dt * D + _h * 128:_dt * D + (_h + 1) * 128],
                                           xnT[:, _dt * S + _ic * 512: _dt * S + (_ic + 1) * 512],
                                           start=(_dt == 0), stop=(_dt == NDT - 1)))
                            if dt == NDT - 1:
                                pe_q = p.op("tensor", mk, P)
                            else:
                                p.opn("tensor", mk)
                        qs = q_sb2[ic % 2]
                        p.wait("scalar", P, pe_q)
                        if ("q_sb", ic % 2) in last:
                            p.wait("scalar", V, last[("q_sb", ic % 2)])
                        a_q = p.opf("scalar", lambda E, _ps=ps_kq, _o=off, _qs=qs: E.activation(
                            _qs[:], _ps[:, _o:_o + 512], AF.Copy))
                        a_q_l[ic] = a_q
                        p.wait("vector", P, pe_q)
                        p.wait("vector", A, a_q)
                        p.op("vector", lambda E, _ic=ic, _qs=qs: E.tensor_tensor(
                            tmpX[:], _qs[:], tcc[:, _ic * 512:(_ic + 1) * 512], ALU.mult))
                        p.op("vector", lambda E, _ic=ic, _qs=qs: E.tensor_tensor(
                            psD_[:, 0:512], _qs[:], tnsns[:, _ic * 512:(_ic + 1) * 512], ALU.mult))
                        for (r_out, xs_, ys_) in ((0, 0, 32), (32, 32, 0), (64, 64, 96), (96, 96, 64)):
                            p.op("vector", lambda E, _ro=r_out, _xs=xs_, _ys=ys_: E.tensor_tensor(
                                rope[_ro:_ro + 32, :], tmpX[_xs:_xs + 32, :], psD_[_ys:_ys + 32, 0:512], ALU.add))
                        p.op("vector", lambda E, _ic=ic: E.tensor_tensor(
                            Qbuf[0][64:128, _ic * 512:(_ic + 1) * 512], rope[0:64, :],
                            vt[0:64, _ic * 512:(_ic + 1) * 512], ALU.add))
                        p.op("vector", lambda E, _ic=ic: E.tensor_tensor(
                            Qbuf[1][64:128, _ic * 512:(_ic + 1) * 512], rope[64:128, :],
                            vt[64:128, _ic * 512:(_ic + 1) * 512], ALU.add))
                        p.op("vector", lambda E, _ic=ic, _h=hp_i, _qs=qs: E.tensor_scalar(
                            Qbuf[0][0:64, _ic * 512:(_ic + 1) * 512], _qs[0:64, :],
                            ucol[0:64, _h:_h + 1], None, ALU.add))
                        v_q = p.op("vector", lambda E, _ic=ic, _h=hp_i, _qs=qs: E.tensor_scalar(
                            Qbuf[1][0:64, _ic * 512:(_ic + 1) * 512], _qs[64:128, :],
                            ucol[64:128, _h:_h + 1], None, ALU.add), V)
                        v_q_ic[ic] = v_q
                        last[("q_sb", ic % 2)] = v_q
                        last[("psKQ", ic)] = (a_q, None)
                    v_q_done = V.n
                    for bi, (h_in_pair, ig) in enumerate(((0, 0), (0, 1), (1, 0), (1, 1))):
                        head = hp_i * 2 + h_in_pair
                        Kb = Kbuf[h_in_pair]
                        Qb = Qbuf[h_in_pair]
                        va = vaug[hp_i % 2][h_in_pair]
                        par = last.get(("blkpar",), 0)
                        ps_c = psC_ if par == 0 else psD_
                        last[("blkpar",)] = 1 - par
                        if ("ps_c", par) in last:
                            v_cbp, a_csp = last[("ps_c", par)]
                            p.wait("tensor", V, v_cbp)
                            p.wait("tensor", A, a_csp)
                        if ("slot_exp",) in last:
                            p.wait("tensor", A, last[("slot_exp",)])
                        if bi == 0:
                            p.wait("tensor", A, max(a_v_done, a_q_l[3]))
                            p.wait("tensor", V, max(v_q_ic[1], v_ones))
                            p.wait("tensor", DG, kb_DG)
                        if bi == 1:
                            p.wait("tensor", V, v_q_ic[3])
                        def s_mm(jt):
                            sl = psA_ if jt % 2 == 0 else psB_
                            pe_s = None
                            for nh in range(2):
                                mk = (lambda E, _jt=jt, _nh=nh, _ig=ig, _sl=sl, _K=Kb, _Q=Qb:
                                      E.matmul(_sl[:, _nh * 512:(_nh + 1) * 512],
                                               _K[:, _jt * 128:(_jt + 1) * 128],
                                               _Q[:, _ig * 1024 + _nh * 512: _ig * 1024 + (_nh + 1) * 512],
                                               start=True, stop=True))
                                if nh == 1:
                                    pe_s = p.op("tensor", mk, P)
                                else:
                                    p.opn("tensor", mk)
                            return pe_s
                        pe_s_l = [None] * 16
                        pe_s_l[0] = s_mm(0)
                        pe_s_l[1] = s_mm(1)
                        pe_c = None
                        a_e_l = [None] * 16
                        for jt in range(16):
                            if jt == 10 and ("pend",) in last:
                                emit_pend()
                            eb = exp_sb[jt % 2]
                            sl = psA_ if jt % 2 == 0 else psB_
                            p.wait("scalar", P, pe_s_l[jt])
                            if ("expw", jt % 2) in last:
                                p.wait("scalar", P, last[("expw", jt % 2)])
                            a_e_l[jt] = p.opf("scalar", lambda E, _eb=eb, _sl=sl:
                                              E.activation(_eb[:], _sl[:], AF.Exp, bias=cst[:, 1:2]))
                            p.wait("tensor", A, a_e_l[jt])
                            for nh in range(2):
                                mk = (lambda E, _jt=jt, _nh=nh, _eb=eb, _pc=ps_c, _va=va:
                                      E.matmul(_pc[0:65, _nh * 512:(_nh + 1) * 512],
                                               _va[:, _jt * 65:(_jt + 1) * 65],
                                               _eb[:, _nh * 512:(_nh + 1) * 512],
                                               start=(_jt == 0), stop=(_jt == 15)))
                                if nh == 1:
                                    pe_c = p.op("tensor", mk, P)
                                else:
                                    p.opn("tensor", mk)
                            last[("expw", jt % 2)] = P.n
                            if jt + 2 < 16:
                                pe_s_l[jt + 2] = s_mm(jt + 2)
                        last[("slot_exp",)] = a_e_l[15]
                        if ("rec", par) in last:
                            p.wait("vector", P, last[("rec", par)])
                        p.wait("vector", P, pe_c)
                        v_r = p.opf("vector", lambda E, _pc=ps_c, _par=par:
                                    E.reciprocal(rec_row[_par][:], _pc[64:65, :]))
                        p.wait("scalar", P, pe_c)
                        if ("ctxst", par) in last:
                            p.wait("scalar", V, last[("ctxst", par)])
                        a_cs = p.opf("scalar", lambda E, _pc=ps_c, _par=par:
                                     E.activation(ctx_stage[_par][:], _pc[0:64, :], AF.Copy))
                        last[("pend",)] = (ps_c, par, v_r, a_cs, head, ig)
                if ("pend",) in last:
                    emit_pend()
                p.wait("tensor", A, last[("slot_exp",)])
                v_ctx_done = V.n
                v_o = None
                for ic in range(4):
                    if ("sTst", 0) in last:
                        p.wait("vector", sst, last[("sTst", 0)])
                    for dmt in range(NDT):
                        half = dmt % 2
                        ps_o = psA_ if half == 0 else psB_
                        if ("psWo", half) in last:
                            p.wait("tensor", V, last[("psWo", half)])
                        p.wait("tensor", V, v_ctx_done)
                        pe_o = None
                        for dct in range(NDT):
                            mk = (lambda E, _dct=dct, _dmt=dmt, _ic=ic, _ps=ps_o:
                                  E.matmul(_ps[:, 0:512],
                                           woT_b[:, _dct * D + _dmt * 128:_dct * D + (_dmt + 1) * 128],
                                           ctx_buf[:, _dct * S + _ic * 512:_dct * S + (_ic + 1) * 512],
                                           start=(_dct == 0), stop=(_dct == NDT - 1)))
                            if dct == NDT - 1:
                                pe_o = p.op("tensor", mk, P)
                            else:
                                p.opn("tensor", mk)
                        p.wait("vector", P, pe_o)
                        v_o = p.op("vector", lambda E, _dmt=dmt, _ps=ps_o:
                                   E.tensor_scalar(Qbuf[_dmt // 4][:, (_dmt % 4) * 512:((_dmt % 4) + 1) * 512],
                                                   _ps[:, 0:512], bo_c[:, _dmt:_dmt + 1], None, ALU.add), V)
                        last[("psWo", half)] = v_o
                    p.wait("sync", V, v_o)
                    ds_s = None
                    for qb in range(2):
                        ds_s = p.dma("sync",
                                     sT_hbm.rearrange("(dt p) t -> p dt t", p=128)[:, qb * 4:(qb + 1) * 4, base_tok + ic * 512: base_tok + (ic + 1) * 512],
                                     Qbuf[qb].ap().rearrange("p (dt t) -> p dt t", dt=4), sst)
                    last[("sTst", 0)] = ds_s

            for b in range(NB):
                ln_gate_batch(b)
                if stage == "ln":
                    break
                if stage != "moe":
                    attention_batch(b)

            att_P, att_A, att_V, att_DS = P.n, A.n, V.n, DS.n
            if tp_xnT is not None:
                p.wait("sync", sxt, sxt.n)
                p.dma("sync", tp_xnT, xnT_hbm, DO)
            if tp_wgt is not None:
                p.wait("gpsimd", sgw, sgw.n)
                p.dma("gpsimd", tp_wgt, wgt_hbm, DOG)
            if tp_sT is not None and stage != "ln":
                p.wait("sync", sst, sst.n)
                p.dma("sync", tp_sT, sT_hbm, DO)

        if stage in ("moe", "full"):
            es3 = contextlib.ExitStack()
            with es3:
                def sb3(name, shape, dt):
                    return es3.enter_context(nc.sbuf_tensor(name, list(shape), dt))

                acc = sb3("acc", (128, NDT * T), BF16)
                es3b = contextlib.ExitStack()
                es3b.__enter__()
                def sb3b(name, shape, dt):
                    return es3b.enter_context(nc.sbuf_tensor(name, list(shape), dt))
                W1q = [sb3b(f"W1q{i}", (128, NDT * 1024), BF16) for i in range(2)]
                W2q = [sb3b(f"W2q{i}", (128, 8 * D), BF16) for i in range(2)]
                h_w2 = [sb3b(f"h_w{i}", (128, 16 * 512), BF16) for i in range(2)]
                h_sb = [sb3b(f"h_sb{i}", (128, 512), BF16) for i in range(4)]
                xn_tc = [sb3b(f"xn_tc{i}", (128, NDT * 512), BF16) for i in range(2)]
                w_sb = sb3b("w_sb", (128, 8 * 512), BF16)
                wrow = sb3b("wrow", (1, T), BF16)
                wgtT_m = sb3b("wgtT_m", (NE, T), BF16)
                b1_c = [sb3b(f"b1_c{i}", (128, 32), F32) for i in range(2)]
                if os.environ.get("BASS_DEBUG_MEM"):
                    print("SBUF free in moe scope:", nc.sbuf_bytes_remaining)
                psH = [es3b.enter_context(nc.psum_tensor(f"psH{i}", [128, 1024], F32)) for i in range(2)]
                psO = [es3b.enter_context(nc.psum_tensor(f"psO{i}", [128, 1024], F32)) for i in range(2)]

                p.wait("gpsimd", sxt, sxt.n)
                p.wait("gpsimd", sst, sst.n)
                p.wait("gpsimd", sgw, sgw.n)
                p.wait("gpsimd", P, att_P)
                p.wait("gpsimd", A, att_A)
                p.wait("gpsimd", V, att_V)
                p.wait("gpsimd", DS, att_DS)
                p.wait("tensor", A, att_A)
                p.wait("tensor", V, att_V)
                p.wait("vector", A, att_A)
                p.wait("vector", P, att_P)
                p.wait("scalar", V, att_V)
                p.wait("scalar", P, att_P)
                p.wait("sync", P, att_P)
                p.wait("sync", V, att_V)
                p.wait("sync", A, att_A)

                ev = {}
                d_mw = p.dma("gpsimd", wgtT_m[:], wgt_hbm, smw)
                p.wait("tensor", smw, d_mw)

                NSL = NE * 4

                def w_slice_dma(s):
                    e_, fq_ = s // 4, s % 4
                    if ("W1use", s % 2) in ev:
                        p.wait("gpsimd", P, ev[("W1use", s % 2)])
                    d1 = p.dma("gpsimd",
                               W1q[s % 2].ap().rearrange("p (dt f) -> p dt f", dt=NDT),
                               w1_d[e_].rearrange("(dt p) f -> p dt f", p=128)[:, :, fq_ * 1024:(fq_ + 1) * 1024],
                               w1s[s % 2])
                    if ("W2use", s % 2) in ev:
                        p.wait("gpsimd", P, ev[("W2use", s % 2)])
                    d2 = p.dma("gpsimd",
                               W2q[s % 2].ap().rearrange("p (ft dm) -> p ft dm", ft=8),
                               w2_d[e_].rearrange("(ft p) dm -> p ft dm", p=128)[:, fq_ * 8:(fq_ + 1) * 8, :],
                               w2s[s % 2])
                    ev[("Wdma", s)] = (d1, d2)

                a_i = None
                for tc in range(8):
                    for dmt in range(NDT):
                        k = (tc * NDT + dmt) % 4
                        ps = psO[k // 2]
                        sl = (k % 2) * 512
                        if ("psOin", k) in ev:
                            p.wait("tensor", A, ev[("psOin", k)])
                        pe_i = p.op("tensor", lambda E, _dmt=dmt, _tc=tc, _ps=ps, _sl=sl:
                                    E.matmul(_ps[:, _sl:_sl + 512],
                                             b2_b[:, _dmt * 128:(_dmt + 1) * 128],
                                             wgtT_m[:, _tc * 512:(_tc + 1) * 512],
                                             start=True, stop=True), P)
                        p.wait("scalar", P, pe_i)
                        a_i = p.opf("scalar", lambda E, _dmt=dmt, _tc=tc, _ps=ps, _sl=sl:
                                    E.activation(acc[:, _dmt * T + _tc * 512:_dmt * T + (_tc + 1) * 512],
                                                 _ps[:, _sl:_sl + 512], AF.Copy))
                        ev[("psOin", k)] = a_i
                psO_last_reader = [(A, a_i), (A, a_i)]
                psH_last_reader = [None, None]

                w_slice_dma(0)
                w_slice_dma(1)

                def w1_stage(s, tc, bb, fq):
                    xb = xn_tc[tc % 2]
                    hw = h_w2[tc % 2]
                    if ("xn", tc % 2) in ev:
                        p.wait("sync", P, ev[("xn", tc % 2)])
                    dg_x = p.dma("sync", xb.ap().rearrange("p (dt t) -> p dt t", dt=NDT),
                                 xnT_hbm.rearrange("(dt p) t -> p dt t", p=128)[:, :, tc * 512:(tc + 1) * 512], sxn[tc % 2])
                    if ("h_w", tc % 2) in ev:
                        p.wait("vector", P, ev[("h_w", tc % 2)])
                    W1c = W1q[s % 2]
                    for ftg in range(4):
                        half = ftg % 2
                        psh = psH[half]
                        if psH_last_reader[half] is not None:
                            p.wait("tensor", psH_last_reader[half][0], psH_last_reader[half][1])
                            psH_last_reader[half] = None
                        if ("psHw", half) in ev:
                            p.wait("tensor", A, ev[("psHw", half)])
                        p.wait("tensor", w1s[s % 2], ev[("Wdma", s)][0])
                        p.wait("tensor", sxn[tc % 2], dg_x)
                        pe_h = None
                        for sub in range(2):
                            ft = ftg * 2 + sub
                            for dt in range(NDT):
                                mk = (lambda E, _dt=dt, _ft=ft, _sub=sub, _ps=psh, _xb=xb, _W=W1c:
                                      E.matmul(_ps[:, _sub * 512:(_sub + 1) * 512],
                                               _W[:, _dt * 1024 + _ft * 128:_dt * 1024 + (_ft + 1) * 128],
                                               _xb[:, _dt * 512:(_dt + 1) * 512],
                                               start=(_dt == 0), stop=(_dt == NDT - 1)))
                                if sub == 1 and dt == NDT - 1:
                                    pe_h = p.op("tensor", mk, P)
                                else:
                                    p.opn("tensor", mk)
                        a_h = None
                        for sub in range(2):
                            ft = ftg * 2 + sub
                            hk = ft % 4
                            p.wait("scalar", P, pe_h)
                            if ("h_sb", hk) in ev:
                                p.wait("scalar", V, ev[("h_sb", hk)])
                            a_h = p.opf("scalar", lambda E, _ft=ft, _fq=fq, _sub=sub, _ps=psh, _hb=h_sb[hk], _bb=bb:
                                        E.activation(_hb[:], _ps[:, _sub * 512:(_sub + 1) * 512], AF.Relu,
                                                     bias=_bb[:, _fq * 8 + _ft:_fq * 8 + _ft + 1]))
                            p.wait("vector", A, a_h)
                            v_h = p.opf("vector", lambda E, _ft=ft, _tc=tc, _hb=h_sb[hk], _hw=hw:
                                        E.tensor_tensor(_hw[:, _ft * 512:(_ft + 1) * 512], _hb[:],
                                                        w_sb[:, _tc * 512:(_tc + 1) * 512], ALU.mult))
                            ev[("h_sb", hk)] = v_h
                        ev[("psHw", half)] = a_h
                    ev[("v_h_done", tc)] = V.n
                    ev[("xn", tc % 2)] = P.n

                def w2_stage(s, tc):
                    hw = h_w2[tc % 2]
                    W2c = W2q[s % 2]
                    v_acc = None
                    for dmg in range(4):
                        half = dmg % 2
                        pso = psO[half]
                        if psO_last_reader[half] is not None:
                            p.wait("tensor", psO_last_reader[half][0], psO_last_reader[half][1])
                            psO_last_reader[half] = None
                        if ("psO2", half) in ev:
                            p.wait("tensor", V, ev[("psO2", half)])
                        p.wait("tensor", V, ev[("v_h_done", tc)])
                        p.wait("tensor", w2s[s % 2], ev[("Wdma", s)][1])
                        pe_o2 = None
                        for sub in range(2):
                            dmt = dmg * 2 + sub
                            for ft in range(8):
                                mk = (lambda E, _ft=ft, _dmt=dmt, _sub=sub, _ps=pso, _hw=hw, _W=W2c:
                                      E.matmul(_ps[:, _sub * 512:(_sub + 1) * 512],
                                               _W[:, _ft * D + _dmt * 128:_ft * D + (_dmt + 1) * 128],
                                               _hw[:, _ft * 512:(_ft + 1) * 512],
                                               start=(_ft == 0), stop=(_ft == 7)))
                                if sub == 1 and ft == 7:
                                    pe_o2 = p.op("tensor", mk, P)
                                else:
                                    p.opn("tensor", mk)
                        p.wait("vector", P, pe_o2)
                        v_acc = p.opf("vector", lambda E, _dmg=dmg, _tc=tc, _ps=pso:
                                     E.tensor_tensor(
                                         acc.ap().rearrange("p (dmt t) -> p dmt t", dmt=NDT)[:, _dmg * 2:_dmg * 2 + 2, _tc * 512:(_tc + 1) * 512],
                                         acc.ap().rearrange("p (dmt t) -> p dmt t", dmt=NDT)[:, _dmg * 2:_dmg * 2 + 2, _tc * 512:(_tc + 1) * 512],
                                         _ps.ap().rearrange("p (s t) -> p s t", s=2),
                                         ALU.add))
                        ev[("psO2", half)] = v_acc
                    ev[("h_w", tc % 2)] = P.n

                for s in range(NSL):
                    e, fq = s // 4, s % 4
                    if fq == 0:
                        if ("wrow",) in ev:
                            p.wait("gpsimd", P, ev[("wrow",)])
                        d_wr = p.dma("gpsimd", wrow[:], wgt_hbm[e:e + 1, :], swr)
                        p.wait("tensor", swr, d_wr)
                        a_w = None
                        for tc in range(8):
                            k = tc % 4
                            ps = psO[k // 2]
                            sl = (k % 2) * 512
                            sem_c = psO_last_reader[k // 2]
                            if sem_c is not None:
                                p.wait("tensor", sem_c[0], sem_c[1])
                                psO_last_reader[k // 2] = None
                            if ("psWb", k) in ev:
                                p.wait("tensor", A, ev[("psWb", k)])
                            pe_w = p.op("tensor", lambda E, _tc=tc, _ps=ps, _sl=sl:
                                        E.matmul(_ps[:, _sl:_sl + 512], ones1[:, 0:128],
                                                 wrow[:, _tc * 512:(_tc + 1) * 512], start=True, stop=True), P)
                            p.wait("scalar", P, pe_w)
                            if ("w_sb", tc) in ev:
                                p.wait("scalar", V, ev[("w_sb", tc)])
                            a_w = p.opf("scalar", lambda E, _tc=tc, _ps=ps, _sl=sl:
                                        E.activation(w_sb[:, _tc * 512:(_tc + 1) * 512], _ps[:, _sl:_sl + 512], AF.Copy))
                            ev[("psWb", k)] = a_w
                        a_w_done = A.n
                        ev[("wrow",)] = P.n
                        psO_last_reader = [(A, a_w_done), (A, a_w_done)]
                        bb = b1_c[e % 2]
                        if ("b1", e % 2) in ev:
                            p.wait("sync", A, ev[("b1", e % 2)])
                        ds_b1 = p.dma("sync", bb[:], b1c_d[e], sb1s[e % 2])
                        p.wait("scalar", sb1s[e % 2], ds_b1)
                        p.wait("scalar", A, a_w_done)
                    if s >= 1 and s + 1 < NSL:
                        w_slice_dma(s + 1)
                    for tc in range(8):
                        w1_stage(s, tc, bb, fq)
                        if tc > 0:
                            w2_stage(s, tc - 1)
                    w2_stage(s, 7)
                    ev[("W1use", s % 2)] = P.n
                    ev[("W2use", s % 2)] = P.n
                    if fq == 3:
                        ev[("b1", e % 2)] = A.n
                        if ("psO2", 0) in ev:
                            psO_last_reader[0] = (V, ev[("psO2", 0)])
                        if ("psO2", 1) in ev:
                            psO_last_reader[1] = (V, ev[("psO2", 1)])
                moe_V = V.n
                es3b.__exit__(None, None, None)

                if tp_acc is not None:
                    accf = sb3("accf", (128, NDT * T), F32)
                    p.wait("vector", V, moe_V)
                    va = p.op("vector", lambda E: E.tensor_copy(accf[:], acc[:]), V)
                    p.wait("sync", V, va)
                    p.dma("sync", tp_acc.rearrange("(dt p) t -> p dt t", p=128),
                          accf.ap().rearrange("p (dt t) -> p dt t", dt=NDT), DO)

                if stage == "full":
                    p.wait("sync", sst, sst.n)
                    p.wait("sync", P, P.n)
                    p.wait("sync", A, A.n)
                    p.wait("sync", V, V.n)
                    for s_ in (sxn[0], sxn[1], w1s[0], w1s[1], w2s[0], w2s[1], swr, smw):
                        p.wait("sync", s_, s_.n)
                    NXB = 4
                    sT_rd = [sb3(f"sT_rd{i}", (128, NDT * 512), BF16) for i in range(2)]
                    g_big = [sb3(f"g_big{i}", (128, NDT * 512), BF16) for i in range(2)]
                    pT_sb2 = [sb3(f"pT_sb{i}", (128, NDT * 512), BF16) for i in range(2)]
                    xo_t = [sb3(f"xo_t{i}", (128, D), F32) for i in range(NXB)]
                    out_t = [sb3(f"out_t{i}", (128, D), F32) for i in range(NXB)]
                    psX = [es3.enter_context(nc.psum_tensor(f"psX{i}", [128, 1024], BF16)) for i in range(2)]
                    if os.environ.get("BASS_DEBUG_MEM"):
                        print("SBUF free in combine scope:", nc.sbuf_bytes_remaining)

                    def srd_load(tcl):
                        return p.dma("sync", sT_rd[tcl % 2].ap().rearrange("p (dt t) -> p dt t", dt=NDT),
                                     sT_hbm.rearrange("(dt p) t -> p dt t", p=128)[:, :, tcl * 512:(tcl + 1) * 512],
                                     ssr[tcl % 2])

                    steps = [(tc_, ttk_) for tc_ in range(8) for ttk_ in range(4)]
                    srd_ev = {0: srd_load(0), 1: srd_load(1)}
                    xo_ev = {}
                    for i in range(NXB):
                        tc_, ttk_ = steps[i]
                        tok0 = tc_ * 512 + ttk_ * 128
                        xo_ev[i] = p.dma("sync", xo_t[i % NXB][:], x_d[tok0:tok0 + 128, :], sxc[i % NXB])
                    for tc in range(8):
                        srd = sT_rd[tc % 2]
                        gb = g_big[tc % 2]
                        p.wait("scalar", ssr[tc % 2], srd_ev[tc])
                        p.wait("scalar", V, moe_V)
                        if ("g_big", tc % 2) in ev:
                            p.wait("scalar", V, ev[("g_big", tc % 2)])
                        a_g = p.opf("scalar", lambda E, _tc=tc, _gb=gb:
                                   E.activation(_gb.ap().rearrange("p (dmt t) -> p dmt t", dmt=NDT),
                                                acc.ap().rearrange("p (dmt t) -> p dmt t", dmt=NDT)[:, :, _tc * 512:(_tc + 1) * 512],
                                                AF.Silu))
                        pT_sb = pT_sb2[tc % 2]
                        if ("pTu", tc % 2) in ev:
                            p.wait("vector", P, ev[("pTu", tc % 2)])
                        p.wait("vector", A, a_g)
                        v_p = p.opf("vector", lambda E, _srd=srd, _gb=gb, _pT=pT_sb:
                                    E.tensor_tensor(_pT[:], _gb[:], _srd[:], ALU.mult))
                        ev[("g_big", tc % 2)] = v_p
                        if tc + 2 < 8:
                            p.wait("sync", V, v_p)
                            srd_ev[tc + 2] = srd_load(tc + 2)
                        for ttk in range(4):
                            i = tc * 4 + ttk
                            half = ttk % 2
                            psx = psX[half]
                            if ("psX", half) in ev:
                                p.wait("tensor", V, ev[("psX", half)])
                            p.wait("tensor", V, v_p)
                            pe_t = None
                            for dmt in range(NDT):
                                mk = (lambda E, _dmt=dmt, _ttk=ttk, _ps=psx, _pT=pT_sb:
                                      E.transpose(_ps[:, _dmt * 128:(_dmt + 1) * 128],
                                                  _pT[:, _dmt * 512 + _ttk * 128:_dmt * 512 + (_ttk + 1) * 128],
                                                  ident[:]))
                                if dmt == NDT - 1:
                                    pe_t = p.op("tensor", mk, P)
                                else:
                                    p.opn("tensor", mk)
                            xo = xo_t[i % NXB]
                            ot = out_t[i % NXB]
                            tok0 = tc * 512 + ttk * 128
                            p.wait("vector", sxc[i % NXB], xo_ev[i])
                            if ("oub", i % NXB) in ev:
                                p.wait("vector", ssoc[i % NXB], ev[("oub", i % NXB)])
                            p.wait("vector", P, pe_t)
                            v_out = p.opf("vector", lambda E, _xo=xo, _ot=ot, _ps=psx:
                                          E.tensor_tensor(_ot[:], _xo[:], _ps[:], ALU.add))
                            ev[("psX", half)] = v_out
                            p.wait("sync", V, v_out)
                            do_out = p.dma("sync", out_d[tok0:tok0 + 128, :], ot[:], ssoc[i % NXB])
                            ev[("oub", i % NXB)] = do_out
                            if i + NXB < 32:
                                tcn, ttkn = steps[i + NXB]
                                tokn = tcn * 512 + ttkn * 128
                                xo_ev[i + NXB] = p.dma("sync", xo_t[i % NXB][:], x_d[tokn:tokn + 128, :], sxc[i % NXB])
                        ev[("pTu", tc % 2)] = P.n

        p.wait("sync", svb, svb.n)
        for s_ in (DO, DOG, DS, DG, sst, sxt, sb1s[0], sb1s[1], swr, sx[0], sx[1], sxn[0], sxn[1],
                   w1s[0], w1s[1], w2s[0], w2s[1], sgw, smw, ssr[0], ssr[1], sso[0], sso[1],
                   sxc[0], sxc[1], sxc[2], sxc[3], ssoc[0], ssoc[1], ssoc[2], ssoc[3],
                   svq[0], svq[1]):
            p.wait("sync", s_, s_.n)

        block = es.enter_context(nc.Block())
        for eng in Prog.ENGINES:
            fns = p.q[eng]
            getattr(block, eng)(lambda E, _fns=fns: [f(E) for f in _fns])

    return tap_tensors


def host_prep(inputs):
    f = np.ascontiguousarray
    perm = np.concatenate([np.arange(0, HD, 2), np.arange(1, HD, 2)])

    def permuted_T(w):
        wp = w.reshape(H, HD, D)[:, perm, :].reshape(D, D)
        return f(wp.T)

    Wq = np.asarray(inputs["L0_attn_Wq"], np.float32)
    Wk = np.asarray(inputs["L0_attn_Wk"], np.float32)
    Wv = np.asarray(inputs["L0_attn_Wv"], np.float32)
    Wo = np.asarray(inputs["L0_attn_Wo"], np.float32)
    u = np.asarray(inputs["L0_attn_u"], np.float32)[:, perm]
    v = np.asarray(inputs["L0_attn_v"], np.float32)[:, perm]
    consts = {
        "wqT": permuted_T(Wq),
        "wkT": permuted_T(Wk),
        "wvT": f(Wv.T),
        "woT": f(Wo.T),
        "boc": f(np.asarray(inputs["L0_attn_bo"], np.float32).reshape(NDT, 128).T),
    }
    ucol = np.zeros((128, HP), np.float32)
    vc1 = np.zeros((128, HP), np.float32)
    vc2 = np.zeros((128, HP), np.float32)
    for hp_i in range(HP):
        he, ho = 2 * hp_i, 2 * hp_i + 1
        ucol[:, hp_i] = np.concatenate([u[he], u[ho]])
        vc1[:, hp_i] = np.concatenate([v[he], v[ho]])
        vc2[:, hp_i] = np.concatenate([v[he][32:], v[he][:32], v[ho][32:], v[ho][:32]])
    consts["ucol"] = ucol
    consts["vc1"] = vc1
    consts["vc2"] = vc2
    m = np.arange(32, dtype=np.float64)
    fm = 10000.0 ** (-m / 32.0)
    i = np.arange(S, dtype=np.float64)
    ang = i[None, :] * fm[:, None]
    cos_t = np.cos(ang).astype(np.float32)
    sin_t = np.sin(ang).astype(np.float32)
    consts["tcc"] = f(np.tile(cos_t, (4, 1)))
    consts["tnsns"] = f(np.concatenate([-sin_t, sin_t, -sin_t, sin_t], 0))
    consts["pab"] = f(np.concatenate([sin_t, cos_t], 0))
    consts["wg"] = f(np.asarray(inputs["L0_ffn_Wg"], np.float32))
    consts["w1"] = f(np.asarray(inputs["L0_ffn_W1"], np.float32))
    b1 = np.asarray(inputs["L0_ffn_b1"], np.float32)
    consts["b1c"] = f(b1.reshape(NE, DF // 128, 128).transpose(0, 2, 1))
    consts["w2"] = f(np.asarray(inputs["L0_ffn_W2"], np.float32))
    consts["b2"] = f(np.asarray(inputs["L0_ffn_b2"], np.float32))
    x = np.asarray(inputs["L0_attn_x"], np.float32)
    xs = [f(x[i * NB:(i + 1) * NB].reshape(T, D)) for i in range(8)]
    return consts, xs


def kernel(**inputs):
    import ml_dtypes
    stage = "full"
    taps = ()
    consts, xs = host_prep(inputs)
    consts["ident"] = np.eye(128).astype(ml_dtypes.bfloat16)
    nc = bass.Bass("TRN2", target_bir_lowering=False, debug=False)
    build(nc, stage=stage, taps=taps)
    in_maps = [dict(consts, x=x) for x in xs]
    res = run_bass_kernel_spmd(nc, in_maps, core_ids=list(range(8)))
    outs = [r["out"].reshape(NB, S, D) for r in res.results]
    full = np.concatenate(outs, 0)
    if taps:
        kernel.tap_results = [{k: r["tap_" + k] for k in taps} for r in res.results]
    return full


# version 16
# speedup vs baseline: 1.1047x; 1.1047x over previous
import os
import contextlib
import numpy as np
import concourse.bass as bass
import concourse.mybir as mybir
from concourse.bass_utils import run_bass_kernel_spmd

F32 = mybir.dt.float32
BF16 = mybir.dt.bfloat16
AF = mybir.ActivationFunctionType
ALU = mybir.AluOpType
AX = mybir.AxisListType

NB = 2
S = 2048
D = 1024
H = 16
HD = 64
HP = H // 2
NE = 8
DF = 4096
T = NB * S
NDT = D // 128
EXP_BIAS = -20.0


class Sem:
    def __init__(self, h):
        self.h = h
        self.n = 0


class Prog:
    ENGINES = ("sync", "tensor", "vector", "scalar", "gpsimd")

    def __init__(self):
        self.q = {e: [] for e in self.ENGINES}
        self.selfsync = {}

    def op(self, eng, make, inc=None, inc_k=1):
        s = self.selfsync.get(eng)
        if s is not None:
            if s.n > 0:
                self.q[eng].append(lambda E, _h=s.h, _v=s.n: E.wait_ge(_h, _v))
            inc = s
            inc_k = 1
        if inc is not None:
            inc.n += inc_k
            h, k = inc.h, inc_k
            self.q[eng].append(lambda E, _m=make, _h=h, _k=k: _m(E).then_inc(_h, _k))
            return inc.n
        self.q[eng].append(lambda E, _m=make: _m(E))
        return None

    def opn(self, eng, make):
        assert eng not in self.selfsync
        self.q[eng].append(lambda E, _m=make: _m(E))

    def opf(self, eng, make):
        s = self.selfsync[eng]
        s.n += 1
        self.q[eng].append(lambda E, _m=make, _h=s.h: _m(E).then_inc(_h, 1))
        return s.n

    def wait(self, eng, sem, val):
        self.q[eng].append(lambda E, _h=sem.h, _v=val: E.wait_ge(_h, _v))

    def dma(self, eng, out, in_, inc):
        return self.op(eng, lambda E, _o=out, _i=in_: E.dma_start(out=_o, in_=_i), inc, 16)


def build(nc, stage="full", taps=()):
    p = Prog()
    tap_tensors = {}

    def dram(name, shape, dt, kind):
        return nc.dram_tensor(name, list(shape), dt, kind=kind).ap()

    x_d = dram("x", (T, D), F32, "ExternalInput")
    wqT_d = dram("wqT", (D, D), F32, "ExternalInput")
    wkT_d = dram("wkT", (D, D), F32, "ExternalInput")
    wvT_d = dram("wvT", (D, D), F32, "ExternalInput")
    woT_d = dram("woT", (D, D), F32, "ExternalInput")
    bo_d = dram("boc", (128, NDT), F32, "ExternalInput")
    ucol_d = dram("ucol", (128, HP), F32, "ExternalInput")
    vc1_d = dram("vc1", (128, HP), F32, "ExternalInput")
    vc2_d = dram("vc2", (128, HP), F32, "ExternalInput")
    tcc_d = dram("tcc", (128, S), F32, "ExternalInput")
    tnsns_d = dram("tnsns", (128, S), F32, "ExternalInput")
    pab_d = dram("pab", (64, S), F32, "ExternalInput")
    ident_d = dram("ident", (128, 128), BF16, "ExternalInput")
    wg_d = dram("wg", (D, NE), F32, "ExternalInput")
    w1_d = dram("w1", (NE, D, DF), F32, "ExternalInput")
    b1c_d = dram("b1c", (NE, 128, DF // 128), F32, "ExternalInput")
    w2_d = dram("w2", (NE, DF, D), F32, "ExternalInput")
    b2_d = dram("b2", (NE, D), F32, "ExternalInput")
    out_d = dram("out", (T, D), F32, "ExternalOutput")

    xnT_hbm = dram("xnT_scr", (D, T), BF16, "Internal")
    wvbf_hbm = dram("wvbf_scr", (D, D), BF16, "Internal")
    sT_hbm = dram("sT_scr", (D, T), BF16, "Internal")
    wgt_hbm = dram("wgt_scr", (NE, T), BF16, "Internal")

    def tap(name, shape, dt=F32):
        if name in taps:
            tap_tensors[name] = dram("tap_" + name, shape, dt, "ExternalOutput")
            return tap_tensors[name]
        return None

    tp_xnT = tap("xnT", (D, T), BF16)
    tp_wgt = tap("wgtT", (NE, T))
    tp_sT = tap("sT", (D, T), BF16)
    tp_acc = tap("acc", (D, T))

    es = contextlib.ExitStack()
    with es:
        es.enter_context(nc.allow_low_precision(
            reason="bf16 matmul operands with fp32 accumulation; bf16 staging within measured tolerance"))

        def sb(name, shape, dt):
            return es.enter_context(nc.sbuf_tensor(name, list(shape), dt))

        def sem(name):
            return Sem(es.enter_context(nc.semaphore(name)))

        DS = sem("DS")
        DG = sem("DG")
        DO = sem("DO")
        P = sem("P")
        A = sem("A")
        V = sem("V")
        sx = [sem("sx0"), sem("sx1")]
        sst = sem("sst")
        sxt = sem("sxt")
        sxn = [sem("sxn0"), sem("sxn1")]
        w1s = [sem("w1sa"), sem("w1sb")]
        w2s = [sem("w2sa"), sem("w2sb")]
        sgw = sem("sgw")
        smw = sem("smw")
        sb1s = [sem("sb1a"), sem("sb1b")]
        swr = sem("swr")
        ssr = [sem("ssr0"), sem("ssr1")]
        sso = [sem("sso0"), sem("sso1")]
        DOG = sem("DOG")
        sxc = [sem(f"sxc{i}") for i in range(4)]
        svq = [sem(f"svq{i}") for i in range(2)]
        svb = sem("svb")
        ssoc = [sem(f"ssoc{i}") for i in range(4)]
        p.selfsync = {"vector": V, "scalar": A}

        ident = sb("ident_s", (128, 128), BF16)
        bo_c = sb("bo_c", (128, NDT), F32)
        b2_b = sb("b2_b", (NE, D), BF16)
        ones1 = sb("ones1", (1, 128), BF16)
        cst = sb("cst", (128, 2), F32)

        p.dma("sync", ident[:], ident_d, DS)
        p.dma("sync", bo_c[:], bo_d, DS)
        p.dma("gpsimd", b2_b[:], b2_d, DG)
        p.op("vector", lambda E: E.memset(ones1[:], 1.0), V)
        p.op("vector", lambda E: E.memset(cst[:, 0:1], 1e-5), V)
        v_cst = p.op("vector", lambda E: E.memset(cst[:, 1:2], EXP_BIAS), V)
        p.wait("scalar", V, v_cst)

        last = {}

        es2 = contextlib.ExitStack()
        dg_xn = [None, None]
        with es2:
            def sb2(name, shape, dt):
                return es2.enter_context(nc.sbuf_tensor(name, list(shape), dt))

            wqT_b = sb2("wqT_b", (128, NDT * D), BF16)
            wkT_b = sb2("wkT_b", (128, NDT * D), BF16)
            woT_b = sb2("woT_b", (128, NDT * D), BF16)
            tcc = sb2("tcc_s", (128, S), BF16)
            tnsns = sb2("tnsns_s", (128, S), BF16)
            ucol = sb2("ucol_s", (128, HP), F32)
            vc1 = sb2("vc1_s", (128, HP), F32)
            vc2 = sb2("vc2_s", (128, HP), F32)
            wg_f = sb2("wg_f", (128, NDT * NE), F32)
            wg_hi = sb2("wg_hi", (128, NDT * NE), BF16)
            wg_lo = sb2("wg_lo", (128, NDT * NE), BF16)
            xt = [sb2(f"xt{i}", (128, D), F32) for i in range(2)]
            xnf_t = sb2("xnf_t", (128, D), F32)
            xnhi_t = [sb2(f"xnhi{i}", (128, D), BF16) for i in range(2)]
            xnlo_t = [sb2(f"xnlo{i}", (128, D), BF16) for i in range(2)]
            wgt_b = sb2("wgt_b", (NE, S), BF16)
            cols = sb2("cols", (128, 8), F32)
            bnst = sb2("bnst", (128, 12), F32)
            loT_t = [sb2(f"loT{i}", (128, NDT * 128), BF16) for i in range(2)]
            g_m1 = sb2("g_m1", (128, 1), F32)
            g_m2 = sb2("g_m2", (128, 1), F32)
            g_msk1 = sb2("g_msk1", (128, NE), F32)
            g_msk2 = sb2("g_msk2", (128, NE), F32)
            g_t2 = sb2("g_t2", (128, NE), F32)
            g_d = sb2("g_d", (128, 1), F32)
            g_e = sb2("g_e", (128, 1), F32)
            g_r = sb2("g_r", (128, 1), F32)
            g_w2 = sb2("g_w2", (128, 1), F32)
            g_t4 = sb2("g_t4", (128, NE), F32)
            g_wgt = sb2("g_wgt", (128, NE), BF16)
            xnT = sb2("xnT", (128, NDT * S), BF16)
            Kbuf = [sb2(f"Kbuf{i}", (128, S), BF16) for i in range(2)]
            Qbuf = [sb2(f"Qbuf{i}", (128, S), BF16) for i in range(2)]
            vaug = [[sb2(f"vaug{g}_{i}", (128, 16 * 65), BF16) for i in range(2)] for g in range(2)]
            vt = sb2("vt", (128, S), BF16)
            q_sb2 = [sb2(f"q_sb{i}", (128, 512), BF16) for i in range(2)]
            tmpX = sb2("tmpX", (128, 512), BF16)
            rope = sb2("rope", (128, 512), F32)
            exp_sb = [sb2(f"exp{i}", (128, 1024), BF16) for i in range(2)]
            rec_row = [sb2(f"rec_row{i}", (1, 1024), BF16) for i in range(2)]
            ctx_stage = [sb2(f"ctx_stage{i}", (64, 1024), BF16) for i in range(2)]
            ctx_buf = sb2("ctx_buf", (128, NDT * S), BF16)

            psA_ = es2.enter_context(nc.psum_tensor("psA", [128, 1024], F32))
            psB_ = es2.enter_context(nc.psum_tensor("psB", [128, 1024], F32))
            psC_ = es2.enter_context(nc.psum_tensor("psC", [128, 1024], F32))
            psD_ = es2.enter_context(nc.psum_tensor("psD", [128, 1024], F32))
            if os.environ.get("BASS_DEBUG_MEM"):
                print("SBUF free in attn scope:", nc.sbuf_bytes_remaining)
            psT = [psA_.ap().bitcast(BF16)[:, 0:128], psA_.ap().bitcast(BF16)[:, 1024:1152],
                   psB_.ap().bitcast(BF16)[:, 0:128], psB_.ap().bitcast(BF16)[:, 1024:1152]]
            psG = [psC_[:, 0:8], psC_[:, 512:520]]
            psW = psD_.ap().bitcast(BF16)[0:8, 0:128]

            wvq = [sb2(f"wvq{i}", (128, NDT * 256), BF16) for i in range(2)]
            p.dma("gpsimd", wvbf_hbm, wvT_d, svb)
            for w_sb_, w_d_ in ((wqT_b, wqT_d), (wkT_b, wkT_d), (woT_b, woT_d)):
                p.dma("gpsimd", w_sb_.ap().rearrange("p (dt n) -> p dt n", dt=NDT),
                      w_d_.rearrange("(dt p) n -> p dt n", p=128), DG)
            p.dma("gpsimd", tcc[:], tcc_d, DG)
            p.dma("gpsimd", tnsns[:], tnsns_d, DG)
            p.dma("sync", ucol[:], ucol_d, DS)
            p.dma("sync", vc1[:], vc1_d, DS)
            p.dma("sync", vc2[:], vc2_d, DS)
            p.dma("sync", wg_f.ap().rearrange("p (dt e) -> p dt e", dt=NDT),
                  wg_d.rearrange("(dt p) e -> p dt e", p=128), DS)
            DS_ALL = DS.n
            p.wait("vector", DS, DS_ALL)
            p.op("vector", lambda E: E.tensor_copy(wg_hi[:], wg_f[:]), V)
            p.op("vector", lambda E: E.tensor_tensor(wg_lo[:], wg_f[:], wg_hi[:], ALU.subtract), V)
            for kb in Kbuf:
                p.dma("gpsimd", kb[64:128, :], pab_d, DG)
            kb_DG = DG.n
            p.wait("tensor", DG, kb_DG)
            p.wait("tensor", DS, DS_ALL)
            v_ones = None
            for _g in range(2):
                for _h in range(2):
                    v_ones = p.op("vector", lambda E, _gg=_g, _hh=_h: E.memset(
                        vaug[_gg][_hh].ap().rearrange("p (j c) -> p j c", j=16)[:, :, 64:65], 1.0), V)

            def ln_gate_batch(b):
                base_tok = b * S
                p.wait("tensor", A, A.n)
                p.wait("tensor", V, V.n)
                if b > 0:
                    p.wait("scalar", sxt, dg_xn[b - 1])
                    p.wait("vector", sxt, dg_xn[b - 1])
                    p.wait("scalar", sgw, 16 * b)

                def issue_x(tt):
                    bf = tt % 2
                    if ("x", bf) in last:
                        p.wait("sync", V, last[("x", bf)])
                    return p.dma("sync", xt[bf][:], x_d[base_tok + tt * 128: base_tok + (tt + 1) * 128, :], sx[bf])

                dsx = {0: issue_x(0)}
                for tt in range(16):
                    buf = tt % 2
                    xtile = xt[buf]
                    if tt + 1 < 16:
                        dsx[tt + 1] = issue_x(tt + 1)
                    p.wait("vector", sx[buf], dsx[tt])
                    p.op("vector", lambda E, _x=xtile: E.bn_stats(bnst[:, 0:6], _x[:, 0:512]))
                    p.op("vector", lambda E, _x=xtile: E.bn_stats(bnst[:, 6:12], _x[:, 512:1024]))
                    if ("lnA",) in last:
                        p.wait("vector", A, last[("lnA",)])
                    vst = p.op("vector", lambda E: E.bn_aggr(
                        cols[:, 0:2], bnst.ap().rearrange("p (g s) -> p g s", g=2)), V)
                    p.wait("scalar", V, vst)
                    al = p.opf("scalar", lambda E: E.activation(cols[:, 6:7], cols[:, 1:2], AF.Ln, bias=cst[:, 0:1], scale=1.0))
                    last[("lnA",)] = al
                    ast = p.op("scalar", lambda E: E.activation(cols[:, 4:5], cols[:, 6:7], AF.Exp, bias=0.0, scale=-0.5), A)
                    p.wait("vector", A, ast)
                    p.op("vector", lambda E: E.tensor_scalar(cols[:, 5:6], cols[:, 0:1], cols[:, 4:5], -1.0, ALU.mult, ALU.mult))
                    if ("xnsrc", buf) in last:
                        p.wait("vector", P, last[("xnsrc", buf)])
                    xh, xl = xnhi_t[buf], xnlo_t[buf]
                    p.op("vector", lambda E, _x=xtile, _o=xh: E.tensor_scalar(
                        _o[:], _x[:], cols[:, 4:5], cols[:, 5:6], ALU.mult, ALU.add))
                    p.op("vector", lambda E, _x=xtile: E.tensor_scalar(
                        xnf_t[:], _x[:], cols[:, 4:5], cols[:, 5:6], ALU.mult, ALU.add))
                    v_xn = p.op("vector", lambda E, _o=xl, _hi=xh: E.tensor_tensor(_o[:], xnf_t[:], _hi[:], ALU.subtract), V)
                    last[("x", buf)] = v_xn
                    p.wait("tensor", V, v_xn)
                    pe_tr = None
                    for dt in range(NDT):
                        for half in range(2):
                            k4 = (dt * 2 + half) % 4
                            if ("psT", k4) in last:
                                eng_sem, cnt = last[("psT", k4)]
                                p.wait("tensor", eng_sem, cnt)
                            src = xh if half == 0 else xl
                            pe_tr = p.op("tensor", lambda E, _s=src, _dt=dt, _ps=psT[k4]:
                                         E.transpose(_ps, _s[:, _dt * 128:(_dt + 1) * 128], ident[:]), P)
                            if half == 0:
                                p.wait("scalar", P, pe_tr)
                                a_ev = p.opf("scalar", lambda E, _dt=dt, _tt=tt, _ps=psT[k4]:
                                             E.activation(xnT[:, _dt * S + _tt * 128: _dt * S + (_tt + 1) * 128],
                                                          _ps, AF.Copy))
                                last[("psT", k4)] = (A, a_ev)
                            else:
                                p.wait("vector", P, pe_tr)
                                v_ev = p.opf("vector", lambda E, _dt=dt, _b=buf, _ps=psT[k4]:
                                             E.tensor_copy(loT_t[_b][:, _dt * 128:(_dt + 1) * 128], _ps))
                                last[("psT", k4)] = (V, v_ev)
                    last[("xnsrc", buf)] = P.n
                    a_hi_done = A.n
                    v_lo_done = V.n
                    gps = psG[buf]
                    if ("gps", buf) in last:
                        p.wait("tensor", V, last[("gps", buf)])
                    p.wait("tensor", A, a_hi_done)
                    p.wait("tensor", V, v_lo_done)
                    specs = []
                    for lhs_kind, rhs in (("hi", wg_hi), ("hi", wg_lo), ("lo", wg_hi)):
                        for dt in range(NDT):
                            specs.append((lhs_kind, dt, rhs))
                    pe_g = None
                    for si, (lhs_kind, dt, rhs) in enumerate(specs):
                        st_, sp_ = (si == 0), (si == len(specs) - 1)
                        if lhs_kind == "hi":
                            mk = (lambda E, _dt=dt, _tt=tt, _r=rhs, _g=gps, _st=st_, _sp=sp_:
                                  E.matmul(_g, xnT[:, _dt * S + _tt * 128: _dt * S + (_tt + 1) * 128],
                                           _r[:, _dt * NE:(_dt + 1) * NE], start=_st, stop=_sp))
                        else:
                            mk = (lambda E, _dt=dt, _b=buf, _r=rhs, _g=gps, _st=st_, _sp=sp_:
                                  E.matmul(_g, loT_t[_b][:, _dt * 128:(_dt + 1) * 128],
                                           _r[:, _dt * NE:(_dt + 1) * NE], start=_st, stop=_sp))
                        if sp_:
                            pe_g = p.op("tensor", mk, P)
                        else:
                            p.opn("tensor", mk)
                    p.wait("vector", P, pe_g)
                    p.op("vector", lambda E, _g=gps: E.reduce_max(g_m1[:], _g, axis=AX.X))
                    p.op("vector", lambda E, _g=gps: E.tensor_scalar(g_msk1[:], _g, g_m1[:, 0:1], None, ALU.is_ge))
                    vg = p.op("vector", lambda E, _g=gps: E.scalar_tensor_tensor(
                        g_t2[:], g_msk1[:], -1e30, _g, ALU.mult, ALU.add), V)
                    last[("gps", buf)] = vg
                    p.op("vector", lambda E: E.reduce_max(g_m2[:], g_t2[:], axis=AX.X))
                    vg2 = p.op("vector", lambda E: E.tensor_tensor(g_d[:], g_m2[:], g_m1[:], ALU.subtract), V)
                    p.wait("scalar", V, vg2)
                    p.opf("scalar", lambda E: E.activation(g_w2[:], g_d[:], AF.Sigmoid, bias=0.0, scale=1.0))
                    ag = p.opf("scalar", lambda E: E.activation(g_r[:], g_d[:], AF.Sigmoid, bias=0.0, scale=-1.0))
                    p.wait("vector", A, ag)
                    p.op("vector", lambda E: E.tensor_scalar(g_msk2[:], g_t2[:], g_m2[:, 0:1], None, ALU.is_ge))
                    p.op("vector", lambda E: E.tensor_scalar(g_t4[:], g_msk1[:], g_r[:, 0:1], None, ALU.mult))
                    vwg = p.op("vector", lambda E: E.scalar_tensor_tensor(
                        g_wgt[:], g_msk2[:], g_w2[:, 0:1], g_t4[:], ALU.mult, ALU.add), V)
                    p.wait("tensor", V, vwg)
                    if "psW" in last:
                        p.wait("tensor", A, last["psW"])
                    peW = p.op("tensor", lambda E: E.transpose(psW, g_wgt[:], ident[:]), P)
                    p.wait("scalar", P, peW)
                    aW = p.opf("scalar", lambda E, _tt=tt:
                               E.activation(wgt_b[:, _tt * 128: (_tt + 1) * 128], psW, AF.Copy))
                    last["psW"] = aW
                    par = os.environ.get("BASS_PARANOID", "")
                    if "A" in par:
                        p.wait("vector", P, P.n); p.wait("vector", A, A.n)
                    if "B" in par:
                        p.wait("tensor", V, V.n); p.wait("tensor", A, A.n)
                    if "E" in par:
                        p.wait("tensor", V, V.n)
                    if "F" in par:
                        p.wait("tensor", A, A.n)
                    if "C" in par:
                        p.wait("scalar", P, P.n); p.wait("scalar", V, V.n)
                    if "D" in par:
                        p.wait("sync", P, P.n); p.wait("sync", A, A.n); p.wait("sync", V, V.n)
                        p.wait("gpsimd", P, P.n); p.wait("gpsimd", A, A.n); p.wait("gpsimd", V, V.n)
                p.wait("gpsimd", A, A.n)
                dg_xn[b] = p.dma("gpsimd",
                                 xnT_hbm.rearrange("(dt p) t -> p dt t", p=128)[:, :, base_tok:base_tok + S],
                                 xnT.ap().rearrange("p (dt t) -> p dt t", dt=NDT), sxt)
                p.wait("sync", A, A.n)
                p.dma("sync", wgt_hbm[:, base_tok:base_tok + S], wgt_b[:], sgw)

            def emit_pend():
                (t_psc, t_par, t_vr, t_acs, t_head, t_ig) = last[("pend",)]
                p.wait("tensor", V, t_vr)
                p.wait("tensor", A, t_acs)
                pe_b = None
                for nh in range(2):
                    mk = (lambda E, _nh=nh, _pc=t_psc, _par=t_par:
                          E.matmul(_pc[0:64, _nh * 512:(_nh + 1) * 512], ones1[0:1, 0:64],
                                   rec_row[_par][:, _nh * 512:(_nh + 1) * 512], start=True, stop=True))
                    if nh == 1:
                        pe_b = p.op("tensor", mk, P)
                    else:
                        p.opn("tensor", mk)
                p.wait("vector", P, pe_b)
                v_cb = p.opf("vector", lambda E, _pc=t_psc, _par=t_par, _hh=t_head // 2, _tig=t_ig, _ro=64 * (t_head % 2):
                             E.tensor_tensor(
                                 ctx_buf[_ro:_ro + 64, _hh * S + _tig * 1024:_hh * S + (_tig + 1) * 1024],
                                 ctx_stage[_par][:], _pc[0:64, :], ALU.mult))
                last[("ps_c", t_par)] = (v_cb, t_acs)
                last[("ctxst", t_par)] = v_cb
                last[("rec", t_par)] = pe_b
                del last[("pend",)]

            def wv_load(b, qi):
                p.wait("sync", svb, 16)
                if ("wvu", qi % 2) in last:
                    p.wait("sync", P, last[("wvu", qi % 2)])
                last[("wvl", qi % 2)] = p.dma(
                    "sync", wvq[qi % 2].ap().rearrange("p (dt n) -> p dt n", dt=NDT),
                    wvbf_hbm.rearrange("(dt p) n -> p dt n", p=128)[:, :, qi * 256:(qi + 1) * 256],
                    svq[qi % 2])

            def attention_batch(b):
                base_tok = b * S
                p.wait("tensor", A, A.n)
                p.wait("tensor", V, V.n)
                wv_load(b, 0)
                wv_load(b, 1)
                if ("sTst", 0) in last:
                    p.wait("vector", sst, last[("sTst", 0)])
                for hp_i in range(HP):
                    if ("slot_exp",) in last:
                        p.wait("tensor", A, last[("slot_exp",)])
                    for ic in range(4):
                        ps_kq = psA_ if ic < 2 else psB_
                        off = (ic % 2) * 512
                        if ("psKQ", ic) in last:
                            a_c, v_c = last[("psKQ", ic)]
                            if a_c is not None:
                                p.wait("tensor", A, a_c)
                            if v_c is not None:
                                p.wait("tensor", V, v_c)
                        pe_k = None
                        for dt in range(NDT):
                            mk = (lambda E, _dt=dt, _h=hp_i, _ic=ic, _ps=ps_kq, _o=off:
                                  E.matmul(_ps[:, _o:_o + 512],
                                           wkT_b[:, _dt * D + _h * 128:_dt * D + (_h + 1) * 128],
                                           xnT[:, _dt * S + _ic * 512: _dt * S + (_ic + 1) * 512],
                                           start=(_dt == 0), stop=(_dt == NDT - 1)))
                            if dt == NDT - 1:
                                pe_k = p.op("tensor", mk, P)
                            else:
                                p.opn("tensor", mk)
                        p.wait("scalar", P, pe_k)
                        a_k = p.opf("scalar", lambda E, _ic=ic, _ps=ps_kq, _o=off:
                                    E.activation(Kbuf[0][0:64, _ic * 512:(_ic + 1) * 512],
                                                 _ps[0:64, _o:_o + 512], AF.Copy))
                        p.wait("vector", P, pe_k)
                        v_k = p.opf("vector", lambda E, _ic=ic, _ps=ps_kq, _o=off:
                                    E.tensor_copy(Kbuf[1][0:64, _ic * 512:(_ic + 1) * 512],
                                                  _ps[64:128, _o:_o + 512]))
                        last[("psKQ", ic)] = (a_k, v_k)
                    if ("pend",) in last:
                        emit_pend()
                    for icq in range(4):
                        p.op("vector", lambda E, _h=hp_i, _i=icq: E.tensor_scalar(
                            vt[:, _i * 512:(_i + 1) * 512], tcc[:, _i * 512:(_i + 1) * 512],
                            vc1[:, _h:_h + 1], None, ALU.mult))
                        p.op("vector", lambda E, _h=hp_i, _i=icq: E.tensor_scalar(
                            tmpX[:], tnsns[:, _i * 512:(_i + 1) * 512], vc2[:, _h:_h + 1], None, ALU.mult))
                        p.op("vector", lambda E, _i=icq: E.tensor_tensor(
                            vt[:, _i * 512:(_i + 1) * 512], vt[:, _i * 512:(_i + 1) * 512], tmpX[:], ALU.subtract))
                    a_q_l = [None] * 4
                    v_q_ic = [None] * 4
                    for ic in range(4):
                        ps_kq = psA_ if ic < 2 else psB_
                        off = (ic % 2) * 512
                        a_c, v_c = last[("psKQ", ic)]
                        if a_c is not None:
                            p.wait("tensor", A, a_c)
                        if v_c is not None:
                            p.wait("tensor", V, v_c)
                        pe_q = None
                        for dt in range(NDT):
                            mk = (lambda E, _dt=dt, _h=hp_i, _ic=ic, _ps=ps_kq, _o=off:
                                  E.matmul(_ps[:, _o:_o + 512],
                                           wqT_b[:, _dt * D + _h * 128:_dt * D + (_h + 1) * 128],
                                           xnT[:, _dt * S + _ic * 512: _dt * S + (_ic + 1) * 512],
                                           start=(_dt == 0), stop=(_dt == NDT - 1)))
                            if dt == NDT - 1:
                                pe_q = p.op("tensor", mk, P)
                            else:
                                p.opn("tensor", mk)
                        qs = q_sb2[ic % 2]
                        p.wait("scalar", P, pe_q)
                        if ("q_sb", ic % 2) in last:
                            p.wait("scalar", V, last[("q_sb", ic % 2)])
                        a_q = p.opf("scalar", lambda E, _ps=ps_kq, _o=off, _qs=qs: E.activation(
                            _qs[:], _ps[:, _o:_o + 512], AF.Copy))
                        a_q_l[ic] = a_q
                        p.wait("vector", P, pe_q)
                        p.wait("vector", A, a_q)
                        p.op("vector", lambda E, _ic=ic, _qs=qs: E.tensor_tensor(
                            tmpX[:], _qs[:], tcc[:, _ic * 512:(_ic + 1) * 512], ALU.mult))
                        p.op("vector", lambda E, _ic=ic, _qs=qs: E.tensor_tensor(
                            psD_[:, 0:512], _qs[:], tnsns[:, _ic * 512:(_ic + 1) * 512], ALU.mult))
                        for (r_out, xs_, ys_) in ((0, 0, 32), (32, 32, 0), (64, 64, 96), (96, 96, 64)):
                            p.op("vector", lambda E, _ro=r_out, _xs=xs_, _ys=ys_: E.tensor_tensor(
                                rope[_ro:_ro + 32, :], tmpX[_xs:_xs + 32, :], psD_[_ys:_ys + 32, 0:512], ALU.add))
                        p.op("vector", lambda E, _ic=ic: E.tensor_tensor(
                            Qbuf[0][64:128, _ic * 512:(_ic + 1) * 512], rope[0:64, :],
                            vt[0:64, _ic * 512:(_ic + 1) * 512], ALU.add))
                        p.op("vector", lambda E, _ic=ic: E.tensor_tensor(
                            Qbuf[1][64:128, _ic * 512:(_ic + 1) * 512], rope[64:128, :],
                            vt[64:128, _ic * 512:(_ic + 1) * 512], ALU.add))
                        p.op("vector", lambda E, _ic=ic, _h=hp_i, _qs=qs: E.tensor_scalar(
                            Qbuf[0][0:64, _ic * 512:(_ic + 1) * 512], _qs[0:64, :],
                            ucol[0:64, _h:_h + 1], None, ALU.add))
                        v_q = p.op("vector", lambda E, _ic=ic, _h=hp_i, _qs=qs: E.tensor_scalar(
                            Qbuf[1][0:64, _ic * 512:(_ic + 1) * 512], _qs[64:128, :],
                            ucol[64:128, _h:_h + 1], None, ALU.add), V)
                        v_q_ic[ic] = v_q
                        last[("q_sb", ic % 2)] = v_q
                        last[("psKQ", ic)] = (a_q, None)
                    if hp_i % 2 == 0:
                        qi = hp_i // 2
                        if ("ps_c", 0) in last:
                            v_cbp0, a_csp0 = last[("ps_c", 0)]
                            p.wait("tensor", V, v_cbp0)
                            p.wait("tensor", A, a_csp0)
                        pe_v = None
                        for jt in range(16):
                            half = jt % 2
                            coff = 512 * half
                            if ("psCv", half) in last:
                                p.wait("tensor", A, last[("psCv", half)])
                            if jt == 0:
                                p.wait("tensor", svq[qi % 2], last[("wvl", qi % 2)])
                            for dt in range(NDT):
                                mk = (lambda E, _dt=dt, _jt=jt, _c=coff, _w=wvq[qi % 2]:
                                      E.matmul(psC_[:, _c:_c + 256],
                                               xnT[:, _dt * S + _jt * 128: _dt * S + (_jt + 1) * 128],
                                               _w[:, _dt * 256:(_dt + 1) * 256],
                                               start=(_dt == 0), stop=(_dt == NDT - 1)))
                                if dt == NDT - 1:
                                    pe_v = p.op("tensor", mk, P)
                                else:
                                    p.opn("tensor", mk)
                            p.wait("scalar", P, pe_v)
                            a_v = None
                            for gp in range(2):
                                p.opf("scalar", lambda E, _jt=jt, _c=coff, _gp=gp:
                                      E.activation(vaug[_gp][0][:, _jt * 65:_jt * 65 + 64],
                                                   psC_[:, _c + _gp * 128:_c + _gp * 128 + 64], AF.Copy))
                                a_v = p.opf("scalar", lambda E, _jt=jt, _c=coff, _gp=gp:
                                            E.activation(vaug[_gp][1][:, _jt * 65:_jt * 65 + 64],
                                                         psC_[:, _c + _gp * 128 + 64:_c + _gp * 128 + 128], AF.Copy))
                            last[("psCv", half)] = a_v
                        last[("wvu", qi % 2)] = pe_v
                        if qi + 2 <= 3:
                            wv_load(b, qi + 2)
                        last[("avdone",)] = A.n
                    a_v_done = last[("avdone",)]
                    v_q_done = V.n
                    for bi, (h_in_pair, ig) in enumerate(((0, 0), (0, 1), (1, 0), (1, 1))):
                        head = hp_i * 2 + h_in_pair
                        Kb = Kbuf[h_in_pair]
                        Qb = Qbuf[h_in_pair]
                        va = vaug[hp_i % 2][h_in_pair]
                        par = last.get(("blkpar",), 0)
                        ps_c = psC_ if par == 0 else psD_
                        last[("blkpar",)] = 1 - par
                        if ("ps_c", par) in last:
                            v_cbp, a_csp = last[("ps_c", par)]
                            p.wait("tensor", V, v_cbp)
                            p.wait("tensor", A, a_csp)
                        if ("slot_exp",) in last:
                            p.wait("tensor", A, last[("slot_exp",)])
                        if bi == 0:
                            p.wait("tensor", A, max(a_v_done, a_q_l[3]))
                            p.wait("tensor", V, max(v_q_ic[1], v_ones))
                            p.wait("tensor", DG, kb_DG)
                        if bi == 1:
                            p.wait("tensor", V, v_q_ic[3])
                        def s_mm(jt):
                            sl = psA_ if jt % 2 == 0 else psB_
                            pe_s = None
                            for nh in range(2):
                                mk = (lambda E, _jt=jt, _nh=nh, _ig=ig, _sl=sl, _K=Kb, _Q=Qb:
                                      E.matmul(_sl[:, _nh * 512:(_nh + 1) * 512],
                                               _K[:, _jt * 128:(_jt + 1) * 128],
                                               _Q[:, _ig * 1024 + _nh * 512: _ig * 1024 + (_nh + 1) * 512],
                                               start=True, stop=True))
                                if nh == 1:
                                    pe_s = p.op("tensor", mk, P)
                                else:
                                    p.opn("tensor", mk)
                            return pe_s
                        pe_s_l = [None] * 16
                        pe_s_l[0] = s_mm(0)
                        pe_s_l[1] = s_mm(1)
                        pe_c = None
                        a_e_l = [None] * 16
                        for jt in range(16):
                            if jt == 10 and ("pend",) in last:
                                emit_pend()
                            eb = exp_sb[jt % 2]
                            sl = psA_ if jt % 2 == 0 else psB_
                            p.wait("scalar", P, pe_s_l[jt])
                            if ("expw", jt % 2) in last:
                                p.wait("scalar", P, last[("expw", jt % 2)])
                            a_e_l[jt] = p.opf("scalar", lambda E, _eb=eb, _sl=sl:
                                              E.activation(_eb[:], _sl[:], AF.Exp, bias=cst[:, 1:2]))
                            p.wait("tensor", A, a_e_l[jt])
                            for nh in range(2):
                                mk = (lambda E, _jt=jt, _nh=nh, _eb=eb, _pc=ps_c, _va=va:
                                      E.matmul(_pc[0:65, _nh * 512:(_nh + 1) * 512],
                                               _va[:, _jt * 65:(_jt + 1) * 65],
                                               _eb[:, _nh * 512:(_nh + 1) * 512],
                                               start=(_jt == 0), stop=(_jt == 15)))
                                if nh == 1:
                                    pe_c = p.op("tensor", mk, P)
                                else:
                                    p.opn("tensor", mk)
                            last[("expw", jt % 2)] = P.n
                            if jt + 2 < 16:
                                pe_s_l[jt + 2] = s_mm(jt + 2)
                        last[("slot_exp",)] = a_e_l[15]
                        if ("rec", par) in last:
                            p.wait("vector", P, last[("rec", par)])
                        p.wait("vector", P, pe_c)
                        v_r = p.opf("vector", lambda E, _pc=ps_c, _par=par:
                                    E.reciprocal(rec_row[_par][:], _pc[64:65, :]))
                        p.wait("scalar", P, pe_c)
                        if ("ctxst", par) in last:
                            p.wait("scalar", V, last[("ctxst", par)])
                        a_cs = p.opf("scalar", lambda E, _pc=ps_c, _par=par:
                                     E.activation(ctx_stage[_par][:], _pc[0:64, :], AF.Copy))
                        last[("pend",)] = (ps_c, par, v_r, a_cs, head, ig)
                if ("pend",) in last:
                    emit_pend()
                p.wait("tensor", A, last[("slot_exp",)])
                v_ctx_done = V.n
                v_o = None
                for ic in range(4):
                    if ("sTst", 0) in last:
                        p.wait("vector", sst, last[("sTst", 0)])
                    for dmt in range(NDT):
                        half = dmt % 2
                        ps_o = psA_ if half == 0 else psB_
                        if ("psWo", half) in last:
                            p.wait("tensor", V, last[("psWo", half)])
                        p.wait("tensor", V, v_ctx_done)
                        pe_o = None
                        for dct in range(NDT):
                            mk = (lambda E, _dct=dct, _dmt=dmt, _ic=ic, _ps=ps_o:
                                  E.matmul(_ps[:, 0:512],
                                           woT_b[:, _dct * D + _dmt * 128:_dct * D + (_dmt + 1) * 128],
                                           ctx_buf[:, _dct * S + _ic * 512:_dct * S + (_ic + 1) * 512],
                                           start=(_dct == 0), stop=(_dct == NDT - 1)))
                            if dct == NDT - 1:
                                pe_o = p.op("tensor", mk, P)
                            else:
                                p.opn("tensor", mk)
                        p.wait("vector", P, pe_o)
                        v_o = p.op("vector", lambda E, _dmt=dmt, _ps=ps_o:
                                   E.tensor_scalar(Qbuf[_dmt // 4][:, (_dmt % 4) * 512:((_dmt % 4) + 1) * 512],
                                                   _ps[:, 0:512], bo_c[:, _dmt:_dmt + 1], None, ALU.add), V)
                        last[("psWo", half)] = v_o
                    p.wait("sync", V, v_o)
                    ds_s = None
                    for qb in range(2):
                        ds_s = p.dma("sync",
                                     sT_hbm.rearrange("(dt p) t -> p dt t", p=128)[:, qb * 4:(qb + 1) * 4, base_tok + ic * 512: base_tok + (ic + 1) * 512],
                                     Qbuf[qb].ap().rearrange("p (dt t) -> p dt t", dt=4), sst)
                    last[("sTst", 0)] = ds_s

            for b in range(NB):
                ln_gate_batch(b)
                if stage == "ln":
                    break
                if stage != "moe":
                    attention_batch(b)

            att_P, att_A, att_V, att_DS = P.n, A.n, V.n, DS.n
            if tp_xnT is not None:
                p.wait("sync", sxt, sxt.n)
                p.dma("sync", tp_xnT, xnT_hbm, DO)
            if tp_wgt is not None:
                p.wait("gpsimd", sgw, sgw.n)
                p.dma("gpsimd", tp_wgt, wgt_hbm, DOG)
            if tp_sT is not None and stage != "ln":
                p.wait("sync", sst, sst.n)
                p.dma("sync", tp_sT, sT_hbm, DO)

        if stage in ("moe", "full"):
            es3 = contextlib.ExitStack()
            with es3:
                def sb3(name, shape, dt):
                    return es3.enter_context(nc.sbuf_tensor(name, list(shape), dt))

                acc = sb3("acc", (128, NDT * T), BF16)
                es3b = contextlib.ExitStack()
                es3b.__enter__()
                def sb3b(name, shape, dt):
                    return es3b.enter_context(nc.sbuf_tensor(name, list(shape), dt))
                W1q = [sb3b(f"W1q{i}", (128, NDT * 1024), BF16) for i in range(2)]
                W2q = [sb3b(f"W2q{i}", (128, 8 * D), BF16) for i in range(2)]
                h_w2 = [sb3b(f"h_w{i}", (128, 16 * 512), BF16) for i in range(2)]
                h_sb = [sb3b(f"h_sb{i}", (128, 512), BF16) for i in range(4)]
                xn_tc = [sb3b(f"xn_tc{i}", (128, NDT * 512), BF16) for i in range(2)]
                w_sb = sb3b("w_sb", (128, 8 * 512), BF16)
                wrow = sb3b("wrow", (1, T), BF16)
                wgtT_m = sb3b("wgtT_m", (NE, T), BF16)
                b1_c = [sb3b(f"b1_c{i}", (128, 32), F32) for i in range(2)]
                if os.environ.get("BASS_DEBUG_MEM"):
                    print("SBUF free in moe scope:", nc.sbuf_bytes_remaining)
                psH = [es3b.enter_context(nc.psum_tensor(f"psH{i}", [128, 1024], F32)) for i in range(2)]
                psO = [es3b.enter_context(nc.psum_tensor(f"psO{i}", [128, 1024], F32)) for i in range(2)]

                p.wait("gpsimd", sxt, sxt.n)
                p.wait("gpsimd", sst, sst.n)
                p.wait("gpsimd", sgw, sgw.n)
                p.wait("gpsimd", P, att_P)
                p.wait("gpsimd", A, att_A)
                p.wait("gpsimd", V, att_V)
                p.wait("gpsimd", DS, att_DS)
                p.wait("tensor", A, att_A)
                p.wait("tensor", V, att_V)
                p.wait("vector", A, att_A)
                p.wait("vector", P, att_P)
                p.wait("scalar", V, att_V)
                p.wait("scalar", P, att_P)
                p.wait("sync", P, att_P)
                p.wait("sync", V, att_V)
                p.wait("sync", A, att_A)

                ev = {}
                d_mw = p.dma("gpsimd", wgtT_m[:], wgt_hbm, smw)
                p.wait("tensor", smw, d_mw)

                NSL = NE * 4

                def w_slice_dma(s):
                    e_, fq_ = s // 4, s % 4
                    if ("W1use", s % 2) in ev:
                        p.wait("gpsimd", P, ev[("W1use", s % 2)])
                    d1 = p.dma("gpsimd",
                               W1q[s % 2].ap().rearrange("p (dt f) -> p dt f", dt=NDT),
                               w1_d[e_].rearrange("(dt p) f -> p dt f", p=128)[:, :, fq_ * 1024:(fq_ + 1) * 1024],
                               w1s[s % 2])
                    if ("W2use", s % 2) in ev:
                        p.wait("gpsimd", P, ev[("W2use", s % 2)])
                    d2 = p.dma("gpsimd",
                               W2q[s % 2].ap().rearrange("p (ft dm) -> p ft dm", ft=8),
                               w2_d[e_].rearrange("(ft p) dm -> p ft dm", p=128)[:, fq_ * 8:(fq_ + 1) * 8, :],
                               w2s[s % 2])
                    ev[("Wdma", s)] = (d1, d2)

                a_i = None
                for tc in range(8):
                    for dmt in range(NDT):
                        k = (tc * NDT + dmt) % 4
                        ps = psO[k // 2]
                        sl = (k % 2) * 512
                        if ("psOin", k) in ev:
                            p.wait("tensor", A, ev[("psOin", k)])
                        pe_i = p.op("tensor", lambda E, _dmt=dmt, _tc=tc, _ps=ps, _sl=sl:
                                    E.matmul(_ps[:, _sl:_sl + 512],
                                             b2_b[:, _dmt * 128:(_dmt + 1) * 128],
                                             wgtT_m[:, _tc * 512:(_tc + 1) * 512],
                                             start=True, stop=True), P)
                        p.wait("scalar", P, pe_i)
                        a_i = p.opf("scalar", lambda E, _dmt=dmt, _tc=tc, _ps=ps, _sl=sl:
                                    E.activation(acc[:, _dmt * T + _tc * 512:_dmt * T + (_tc + 1) * 512],
                                                 _ps[:, _sl:_sl + 512], AF.Copy))
                        ev[("psOin", k)] = a_i
                psO_last_reader = [(A, a_i), (A, a_i)]
                psH_last_reader = [None, None]

                w_slice_dma(0)
                w_slice_dma(1)

                def w1_stage(s, tc, bb, fq):
                    xb = xn_tc[tc % 2]
                    hw = h_w2[tc % 2]
                    if ("xn", tc % 2) in ev:
                        p.wait("sync", P, ev[("xn", tc % 2)])
                    dg_x = p.dma("sync", xb.ap().rearrange("p (dt t) -> p dt t", dt=NDT),
                                 xnT_hbm.rearrange("(dt p) t -> p dt t", p=128)[:, :, tc * 512:(tc + 1) * 512], sxn[tc % 2])
                    if ("h_w", tc % 2) in ev:
                        p.wait("vector", P, ev[("h_w", tc % 2)])
                    W1c = W1q[s % 2]
                    for ftg in range(4):
                        half = ftg % 2
                        psh = psH[half]
                        if psH_last_reader[half] is not None:
                            p.wait("tensor", psH_last_reader[half][0], psH_last_reader[half][1])
                            psH_last_reader[half] = None
                        if ("psHw", half) in ev:
                            p.wait("tensor", A, ev[("psHw", half)])
                        p.wait("tensor", w1s[s % 2], ev[("Wdma", s)][0])
                        p.wait("tensor", sxn[tc % 2], dg_x)
                        pe_h = None
                        for sub in range(2):
                            ft = ftg * 2 + sub
                            for dt in range(NDT):
                                mk = (lambda E, _dt=dt, _ft=ft, _sub=sub, _ps=psh, _xb=xb, _W=W1c:
                                      E.matmul(_ps[:, _sub * 512:(_sub + 1) * 512],
                                               _W[:, _dt * 1024 + _ft * 128:_dt * 1024 + (_ft + 1) * 128],
                                               _xb[:, _dt * 512:(_dt + 1) * 512],
                                               start=(_dt == 0), stop=(_dt == NDT - 1)))
                                if sub == 1 and dt == NDT - 1:
                                    pe_h = p.op("tensor", mk, P)
                                else:
                                    p.opn("tensor", mk)
                        a_h = None
                        for sub in range(2):
                            ft = ftg * 2 + sub
                            hk = ft % 4
                            p.wait("scalar", P, pe_h)
                            if ("h_sb", hk) in ev:
                                p.wait("scalar", V, ev[("h_sb", hk)])
                            a_h = p.opf("scalar", lambda E, _ft=ft, _fq=fq, _sub=sub, _ps=psh, _hb=h_sb[hk], _bb=bb:
                                        E.activation(_hb[:], _ps[:, _sub * 512:(_sub + 1) * 512], AF.Relu,
                                                     bias=_bb[:, _fq * 8 + _ft:_fq * 8 + _ft + 1]))
                            p.wait("vector", A, a_h)
                            v_h = p.opf("vector", lambda E, _ft=ft, _tc=tc, _hb=h_sb[hk], _hw=hw:
                                        E.tensor_tensor(_hw[:, _ft * 512:(_ft + 1) * 512], _hb[:],
                                                        w_sb[:, _tc * 512:(_tc + 1) * 512], ALU.mult))
                            ev[("h_sb", hk)] = v_h
                        ev[("psHw", half)] = a_h
                    ev[("v_h_done", tc)] = V.n
                    ev[("xn", tc % 2)] = P.n

                def w2_stage(s, tc):
                    hw = h_w2[tc % 2]
                    W2c = W2q[s % 2]
                    v_acc = None
                    for dmg in range(4):
                        half = dmg % 2
                        pso = psO[half]
                        if psO_last_reader[half] is not None:
                            p.wait("tensor", psO_last_reader[half][0], psO_last_reader[half][1])
                            psO_last_reader[half] = None
                        if ("psO2", half) in ev:
                            p.wait("tensor", V, ev[("psO2", half)])
                        p.wait("tensor", V, ev[("v_h_done", tc)])
                        p.wait("tensor", w2s[s % 2], ev[("Wdma", s)][1])
                        pe_o2 = None
                        for sub in range(2):
                            dmt = dmg * 2 + sub
                            for ft in range(8):
                                mk = (lambda E, _ft=ft, _dmt=dmt, _sub=sub, _ps=pso, _hw=hw, _W=W2c:
                                      E.matmul(_ps[:, _sub * 512:(_sub + 1) * 512],
                                               _W[:, _ft * D + _dmt * 128:_ft * D + (_dmt + 1) * 128],
                                               _hw[:, _ft * 512:(_ft + 1) * 512],
                                               start=(_ft == 0), stop=(_ft == 7)))
                                if sub == 1 and ft == 7:
                                    pe_o2 = p.op("tensor", mk, P)
                                else:
                                    p.opn("tensor", mk)
                        p.wait("vector", P, pe_o2)
                        v_acc = p.opf("vector", lambda E, _dmg=dmg, _tc=tc, _ps=pso:
                                     E.tensor_tensor(
                                         acc.ap().rearrange("p (dmt t) -> p dmt t", dmt=NDT)[:, _dmg * 2:_dmg * 2 + 2, _tc * 512:(_tc + 1) * 512],
                                         acc.ap().rearrange("p (dmt t) -> p dmt t", dmt=NDT)[:, _dmg * 2:_dmg * 2 + 2, _tc * 512:(_tc + 1) * 512],
                                         _ps.ap().rearrange("p (s t) -> p s t", s=2),
                                         ALU.add))
                        ev[("psO2", half)] = v_acc
                    ev[("h_w", tc % 2)] = P.n

                for s in range(NSL):
                    e, fq = s // 4, s % 4
                    if fq == 0:
                        if ("wrow",) in ev:
                            p.wait("gpsimd", P, ev[("wrow",)])
                        d_wr = p.dma("gpsimd", wrow[:], wgt_hbm[e:e + 1, :], swr)
                        p.wait("tensor", swr, d_wr)
                        a_w = None
                        for tc in range(8):
                            k = tc % 4
                            ps = psO[k // 2]
                            sl = (k % 2) * 512
                            sem_c = psO_last_reader[k // 2]
                            if sem_c is not None:
                                p.wait("tensor", sem_c[0], sem_c[1])
                                psO_last_reader[k // 2] = None
                            if ("psWb", k) in ev:
                                p.wait("tensor", A, ev[("psWb", k)])
                            pe_w = p.op("tensor", lambda E, _tc=tc, _ps=ps, _sl=sl:
                                        E.matmul(_ps[:, _sl:_sl + 512], ones1[:, 0:128],
                                                 wrow[:, _tc * 512:(_tc + 1) * 512], start=True, stop=True), P)
                            p.wait("scalar", P, pe_w)
                            if ("w_sb", tc) in ev:
                                p.wait("scalar", V, ev[("w_sb", tc)])
                            a_w = p.opf("scalar", lambda E, _tc=tc, _ps=ps, _sl=sl:
                                        E.activation(w_sb[:, _tc * 512:(_tc + 1) * 512], _ps[:, _sl:_sl + 512], AF.Copy))
                            ev[("psWb", k)] = a_w
                        a_w_done = A.n
                        ev[("wrow",)] = P.n
                        psO_last_reader = [(A, a_w_done), (A, a_w_done)]
                        bb = b1_c[e % 2]
                        if ("b1", e % 2) in ev:
                            p.wait("sync", A, ev[("b1", e % 2)])
                        ds_b1 = p.dma("sync", bb[:], b1c_d[e], sb1s[e % 2])
                        p.wait("scalar", sb1s[e % 2], ds_b1)
                        p.wait("scalar", A, a_w_done)
                    if s >= 1 and s + 1 < NSL:
                        w_slice_dma(s + 1)
                    for tc in range(8):
                        w1_stage(s, tc, bb, fq)
                        if tc > 0:
                            w2_stage(s, tc - 1)
                    w2_stage(s, 7)
                    ev[("W1use", s % 2)] = P.n
                    ev[("W2use", s % 2)] = P.n
                    if fq == 3:
                        ev[("b1", e % 2)] = A.n
                        if ("psO2", 0) in ev:
                            psO_last_reader[0] = (V, ev[("psO2", 0)])
                        if ("psO2", 1) in ev:
                            psO_last_reader[1] = (V, ev[("psO2", 1)])
                moe_V = V.n
                es3b.__exit__(None, None, None)

                if tp_acc is not None:
                    accf = sb3("accf", (128, NDT * T), F32)
                    p.wait("vector", V, moe_V)
                    va = p.op("vector", lambda E: E.tensor_copy(accf[:], acc[:]), V)
                    p.wait("sync", V, va)
                    p.dma("sync", tp_acc.rearrange("(dt p) t -> p dt t", p=128),
                          accf.ap().rearrange("p (dt t) -> p dt t", dt=NDT), DO)

                if stage == "full":
                    p.wait("sync", sst, sst.n)
                    p.wait("sync", P, P.n)
                    p.wait("sync", A, A.n)
                    p.wait("sync", V, V.n)
                    for s_ in (sxn[0], sxn[1], w1s[0], w1s[1], w2s[0], w2s[1], swr, smw):
                        p.wait("sync", s_, s_.n)
                    NXB = 4
                    sT_rd = [sb3(f"sT_rd{i}", (128, NDT * 512), BF16) for i in range(2)]
                    g_big = [sb3(f"g_big{i}", (128, NDT * 512), BF16) for i in range(2)]
                    pT_sb2 = [sb3(f"pT_sb{i}", (128, NDT * 512), BF16) for i in range(2)]
                    xo_t = [sb3(f"xo_t{i}", (128, D), F32) for i in range(NXB)]
                    out_t = [sb3(f"out_t{i}", (128, D), F32) for i in range(NXB)]
                    psX = [es3.enter_context(nc.psum_tensor(f"psX{i}", [128, 1024], BF16)) for i in range(2)]
                    if os.environ.get("BASS_DEBUG_MEM"):
                        print("SBUF free in combine scope:", nc.sbuf_bytes_remaining)

                    def srd_load(tcl):
                        return p.dma("sync", sT_rd[tcl % 2].ap().rearrange("p (dt t) -> p dt t", dt=NDT),
                                     sT_hbm.rearrange("(dt p) t -> p dt t", p=128)[:, :, tcl * 512:(tcl + 1) * 512],
                                     ssr[tcl % 2])

                    steps = [(tc_, ttk_) for tc_ in range(8) for ttk_ in range(4)]
                    srd_ev = {0: srd_load(0), 1: srd_load(1)}
                    xo_ev = {}
                    for i in range(NXB):
                        tc_, ttk_ = steps[i]
                        tok0 = tc_ * 512 + ttk_ * 128
                        xo_ev[i] = p.dma("sync", xo_t[i % NXB][:], x_d[tok0:tok0 + 128, :], sxc[i % NXB])
                    for tc in range(8):
                        srd = sT_rd[tc % 2]
                        gb = g_big[tc % 2]
                        p.wait("scalar", ssr[tc % 2], srd_ev[tc])
                        p.wait("scalar", V, moe_V)
                        if ("g_big", tc % 2) in ev:
                            p.wait("scalar", V, ev[("g_big", tc % 2)])
                        a_g = p.opf("scalar", lambda E, _tc=tc, _gb=gb:
                                   E.activation(_gb.ap().rearrange("p (dmt t) -> p dmt t", dmt=NDT),
                                                acc.ap().rearrange("p (dmt t) -> p dmt t", dmt=NDT)[:, :, _tc * 512:(_tc + 1) * 512],
                                                AF.Silu))
                        pT_sb = pT_sb2[tc % 2]
                        if ("pTu", tc % 2) in ev:
                            p.wait("vector", P, ev[("pTu", tc % 2)])
                        p.wait("vector", A, a_g)
                        v_p = p.opf("vector", lambda E, _srd=srd, _gb=gb, _pT=pT_sb:
                                    E.tensor_tensor(_pT[:], _gb[:], _srd[:], ALU.mult))
                        ev[("g_big", tc % 2)] = v_p
                        if tc + 2 < 8:
                            p.wait("sync", V, v_p)
                            srd_ev[tc + 2] = srd_load(tc + 2)
                        for ttk in range(4):
                            i = tc * 4 + ttk
                            half = ttk % 2
                            psx = psX[half]
                            if ("psX", half) in ev:
                                p.wait("tensor", V, ev[("psX", half)])
                            p.wait("tensor", V, v_p)
                            pe_t = None
                            for dmt in range(NDT):
                                mk = (lambda E, _dmt=dmt, _ttk=ttk, _ps=psx, _pT=pT_sb:
                                      E.transpose(_ps[:, _dmt * 128:(_dmt + 1) * 128],
                                                  _pT[:, _dmt * 512 + _ttk * 128:_dmt * 512 + (_ttk + 1) * 128],
                                                  ident[:]))
                                if dmt == NDT - 1:
                                    pe_t = p.op("tensor", mk, P)
                                else:
                                    p.opn("tensor", mk)
                            xo = xo_t[i % NXB]
                            ot = out_t[i % NXB]
                            tok0 = tc * 512 + ttk * 128
                            p.wait("vector", sxc[i % NXB], xo_ev[i])
                            if ("oub", i % NXB) in ev:
                                p.wait("vector", ssoc[i % NXB], ev[("oub", i % NXB)])
                            p.wait("vector", P, pe_t)
                            v_out = p.opf("vector", lambda E, _xo=xo, _ot=ot, _ps=psx:
                                          E.tensor_tensor(_ot[:], _xo[:], _ps[:], ALU.add))
                            ev[("psX", half)] = v_out
                            p.wait("sync", V, v_out)
                            do_out = p.dma("sync", out_d[tok0:tok0 + 128, :], ot[:], ssoc[i % NXB])
                            ev[("oub", i % NXB)] = do_out
                            if i + NXB < 32:
                                tcn, ttkn = steps[i + NXB]
                                tokn = tcn * 512 + ttkn * 128
                                xo_ev[i + NXB] = p.dma("sync", xo_t[i % NXB][:], x_d[tokn:tokn + 128, :], sxc[i % NXB])
                        ev[("pTu", tc % 2)] = P.n

        p.wait("sync", svb, svb.n)
        for s_ in (DO, DOG, DS, DG, sst, sxt, sb1s[0], sb1s[1], swr, sx[0], sx[1], sxn[0], sxn[1],
                   w1s[0], w1s[1], w2s[0], w2s[1], sgw, smw, ssr[0], ssr[1], sso[0], sso[1],
                   sxc[0], sxc[1], sxc[2], sxc[3], ssoc[0], ssoc[1], ssoc[2], ssoc[3],
                   svq[0], svq[1]):
            p.wait("sync", s_, s_.n)

        block = es.enter_context(nc.Block())
        for eng in Prog.ENGINES:
            fns = p.q[eng]
            getattr(block, eng)(lambda E, _fns=fns: [f(E) for f in _fns])

    return tap_tensors


def host_prep(inputs):
    f = np.ascontiguousarray
    perm = np.concatenate([np.arange(0, HD, 2), np.arange(1, HD, 2)])

    def permuted_T(w):
        wp = w.reshape(H, HD, D)[:, perm, :].reshape(D, D)
        return f(wp.T)

    Wq = np.asarray(inputs["L0_attn_Wq"], np.float32)
    Wk = np.asarray(inputs["L0_attn_Wk"], np.float32)
    Wv = np.asarray(inputs["L0_attn_Wv"], np.float32)
    Wo = np.asarray(inputs["L0_attn_Wo"], np.float32)
    u = np.asarray(inputs["L0_attn_u"], np.float32)[:, perm]
    v = np.asarray(inputs["L0_attn_v"], np.float32)[:, perm]
    consts = {
        "wqT": permuted_T(Wq),
        "wkT": permuted_T(Wk),
        "wvT": f(Wv.T),
        "woT": f(Wo.T),
        "boc": f(np.asarray(inputs["L0_attn_bo"], np.float32).reshape(NDT, 128).T),
    }
    ucol = np.zeros((128, HP), np.float32)
    vc1 = np.zeros((128, HP), np.float32)
    vc2 = np.zeros((128, HP), np.float32)
    for hp_i in range(HP):
        he, ho = 2 * hp_i, 2 * hp_i + 1
        ucol[:, hp_i] = np.concatenate([u[he], u[ho]])
        vc1[:, hp_i] = np.concatenate([v[he], v[ho]])
        vc2[:, hp_i] = np.concatenate([v[he][32:], v[he][:32], v[ho][32:], v[ho][:32]])
    consts["ucol"] = ucol
    consts["vc1"] = vc1
    consts["vc2"] = vc2
    m = np.arange(32, dtype=np.float64)
    fm = 10000.0 ** (-m / 32.0)
    i = np.arange(S, dtype=np.float64)
    ang = i[None, :] * fm[:, None]
    cos_t = np.cos(ang).astype(np.float32)
    sin_t = np.sin(ang).astype(np.float32)
    consts["tcc"] = f(np.tile(cos_t, (4, 1)))
    consts["tnsns"] = f(np.concatenate([-sin_t, sin_t, -sin_t, sin_t], 0))
    consts["pab"] = f(np.concatenate([sin_t, cos_t], 0))
    consts["wg"] = f(np.asarray(inputs["L0_ffn_Wg"], np.float32))
    consts["w1"] = f(np.asarray(inputs["L0_ffn_W1"], np.float32))
    b1 = np.asarray(inputs["L0_ffn_b1"], np.float32)
    consts["b1c"] = f(b1.reshape(NE, DF // 128, 128).transpose(0, 2, 1))
    consts["w2"] = f(np.asarray(inputs["L0_ffn_W2"], np.float32))
    consts["b2"] = f(np.asarray(inputs["L0_ffn_b2"], np.float32))
    x = np.asarray(inputs["L0_attn_x"], np.float32)
    xs = [f(x[i * NB:(i + 1) * NB].reshape(T, D)) for i in range(8)]
    return consts, xs


def kernel(**inputs):
    import ml_dtypes
    stage = "full"
    taps = ()
    consts, xs = host_prep(inputs)
    consts["ident"] = np.eye(128).astype(ml_dtypes.bfloat16)
    nc = bass.Bass("TRN2", target_bir_lowering=False, debug=False)
    build(nc, stage=stage, taps=taps)
    in_maps = [dict(consts, x=x) for x in xs]
    res = run_bass_kernel_spmd(nc, in_maps, core_ids=list(range(8)))
    outs = [r["out"].reshape(NB, S, D) for r in res.results]
    full = np.concatenate(outs, 0)
    if taps:
        kernel.tap_results = [{k: r["tap_" + k] for k in taps} for r in res.results]
    return full
